# version 17
# baseline (speedup 1.0000x reference)
import numpy as np
import ml_dtypes
import concourse.bass as bass
import concourse.mybir as mybir
from concourse.bass_utils import run_bass_kernel_spmd

F32 = mybir.dt.float32
BF16 = mybir.dt.bfloat16
I32 = mybir.dt.int32
U8 = mybir.dt.uint8
AF = mybir.ActivationFunctionType
ALU = mybir.AluOpType

S, E, H, G, D = 2048, 2048, 16, 4, 128
NSEQ = 2
NT = 16
NMT = 4
NEC = E // 128
SCALE = 1.0 / float(np.sqrt(128.0))
SHIFT = -12.0
EPS = 1e-5
INV_E = 1.0 / float(E)


class B:
    def __init__(self, debug=False):
        self.debug = debug
        nc = self.nc = bass.Bass("TRN2", target_bir_lowering=False)
        self._guards = []
        self.sems = {}
        self.cnt = {}
        self.lastwait = {}
        self.ops = {e: [] for e in "tvagy"}
        self.eng_dirty = {e: set() for e in "vag"}
        self.eng_reads = {e: set() for e in "vag"}
        semnames = ["t", "v", "a", "g", "msc", "stq_h", "stq_l", "stx",
                    "sty", "sty1", "sty2", "sty3",
                    "st_wf", "st_jh", "st_jl", "st_dbg"]
        semnames += ["x1m%d" % i for i in range(4)]
        semnames += ["wqs0", "wqs1", "kvs0", "kvs1", "qs0", "qs1",
                     "wfs0", "wfs1"]
        for name in semnames:
            cm = nc.semaphore(name + "_sem")
            self.sems[name] = cm.__enter__()
            self._guards.append(cm)
            self.cnt[name] = 0
        self.declare_dram()
        self.alloc_sbuf()
        self.alloc_psum()
        self.arena_free = None
        self.x1m_tok = [None] * 4
        self.sphi_tok = None
        self.splo_tok = None
        self.scr_tok = {"A": None, "B": None}
        self.par_tok = {}
        self.bstage_tok = None
        self.i32_tok = None
        self.kv_read = None
        self.wq_slot = {}
        self.kv_slot = {}
        self.qr_slot = {}
        self.wf_slot = {}

    @staticmethod
    def _inter(a, b):
        if "*" in a:
            return bool(b)
        if "*" in b:
            return bool(a)
        return bool(a & b)

    def emit(self, eng, fn, waits=(), inc=None, inc_n=1, drain=True,
             reads=None, writes=None):
        if eng in ("v", "a", "g"):
            rd = set(reads) if reads is not None else {"*"}
            wr = set(writes) if writes is not None else {"*"}
            dirty = self.eng_dirty[eng]
            rdset = self.eng_reads[eng]
            need = (self._inter(rd, dirty) or self._inter(wr, dirty)
                    or self._inter(wr, rdset))
            if drain and need and self.ops[eng]:
                self.ops[eng].append(lambda engine: engine.drain())
                dirty.clear()
                rdset.clear()
            dirty |= wr
            rdset |= rd
        merged = {}
        for tok in waits:
            if tok is None:
                continue
            sn, val = tok
            if val <= 0:
                continue
            assert val <= self.cnt[sn], f"wait-on-future {sn}:{val}>{self.cnt[sn]}"
            merged[sn] = max(merged.get(sn, 0), val)
        wl = []
        for sn, val in merged.items():
            if val > self.lastwait.get((eng, sn), 0):
                wl.append((self.sems[sn], val))
                self.lastwait[(eng, sn)] = val
        tok = None
        sem = None
        if inc is not None:
            self.cnt[inc] += inc_n
            tok = (inc, self.cnt[inc])
            sem = self.sems[inc]

        def op(engine, wl=wl, fn=fn, sem=sem, inc_n=inc_n):
            for sh, v in wl:
                engine.wait_ge(sh, v)
            ins = fn(engine)
            if sem is not None:
                ins.then_inc(sem, inc_n)

        self.ops[eng].append(op)
        return tok

    def load(self, dst, src, sem, waits=()):
        return self.emit("y", lambda e, d=dst, s=src: e.dma_start(out=d, in_=s),
                         waits, inc=sem, inc_n=16)

    def store(self, dst, src, sem, waits=()):
        return self.emit("a", lambda e, d=dst, s=src: e.dma_start(out=d, in_=s),
                         waits, inc=sem, inc_n=16, drain=False,
                         reads=(), writes=())

    def tv(self, fn, waits=(), inc="v", reads=None, writes=None):
        return self.emit("v", fn, waits, inc=inc, reads=reads, writes=writes)

    def ta(self, fn, waits=(), inc="a", reads=None, writes=None):
        return self.emit("a", fn, waits, inc=inc, reads=reads, writes=writes)

    def tt(self, fn, waits=(), inc="t"):
        return self.emit("t", fn, waits, inc=inc)

    def vgap(self):
        pass

    def declare_dram(self):
        nc = self.nc
        self.x_h = nc.dram_tensor("x", [NSEQ, S, E], F32, kind="ExternalInput").ap()
        self.wq_h = nc.dram_tensor("wq", [H, E, D], F32, kind="ExternalInput").ap()
        self.bq_h = nc.dram_tensor("bq", [H, D], F32, kind="ExternalInput").ap()
        self.wk_h = nc.dram_tensor("wk", [G, E, D], F32, kind="ExternalInput").ap()
        self.bk_h = nc.dram_tensor("bk", [G, D], F32, kind="ExternalInput").ap()
        self.wv_h = nc.dram_tensor("wv", [G, E, D], F32, kind="ExternalInput").ap()
        self.bv_h = nc.dram_tensor("bv", [G, D], F32, kind="ExternalInput").ap()
        self.wf_h = nc.dram_tensor("wf", [E, E], F32, kind="ExternalInput").ap()
        self.y_h = nc.dram_tensor("y", [NSEQ, S, E], F32, kind="ExternalOutput").ap()
        kd = "ExternalOutput" if self.debug else "Internal"
        self.wqh_d = nc.dram_tensor("wqh_d", [128, H * 2048], BF16, kind=kd).ap()
        self.wql_d = nc.dram_tensor("wql_d", [128, H * 2048], BF16,
                                    kind="Internal").ap()
        self.wkh_d = nc.dram_tensor("wkh_d", [128, G * 2048], BF16,
                                    kind="Internal").ap()
        self.wkl_d = nc.dram_tensor("wkl_d", [128, G * 2048], BF16,
                                    kind="Internal").ap()
        self.wvh_d = nc.dram_tensor("wvh_d", [128, G * 2048], BF16,
                                    kind="Internal").ap()
        self.wvl_d = nc.dram_tensor("wvl_d", [128, G * 2048], BF16,
                                    kind="Internal").ap()
        self.wfT_d = nc.dram_tensor("wfT_d", [128, E * 16], BF16, kind=kd).ap()
        self.qh_d = nc.dram_tensor("qh_d", [NSEQ, 128, H * 2048], BF16,
                                   kind=kd).ap()
        self.ql_d = nc.dram_tensor("ql_d", [NSEQ, 128, H * 2048], BF16,
                                   kind="Internal").ap()
        self.x1_d = nc.dram_tensor("x1_d", [NSEQ, NT, 128, E], F32, kind=kd).ap()
        self.dbg_d = nc.dram_tensor("dbg_d", [128, 80], F32, kind=kd).ap()
        self.dbg2_d = nc.dram_tensor("dbg2_d", [NT, 128, E], BF16, kind=kd).ap()
        self.dbg3_d = nc.dram_tensor("dbg3_d", [128, E], F32, kind=kd).ap()

    def alloc_sbuf(self):
        nc = self.nc
        A = nc.alloc_sbuf_tensor
        self.kch = A("kch", [128, G * S], BF16).ap()
        self.kcl = A("kcl", [128, G * S], BF16).ap()
        self.vch = A("vch", [128, G * S], BF16).ap()
        self.vcl = A("vcl", [128, G * S], BF16).ap()
        self.bqT = A("bqT", [128, H], F32).ap()
        self.bkT = A("bkT", [128, G], F32).ap()
        self.bvT = A("bvT", [128, G], F32).ap()
        self.identb = A("identb", [128, 128], BF16).ap()
        self.identf = A("identf", [128, 128], F32).ap()
        self.ohb = A("ohb", [128, H * 16], BF16).ap()
        self.mk = A("mk", [128, 4 * 512], BF16).ap()
        self.onesb = A("onesb", [128, 128], BF16).ap()
        self.wsc = A("wsc", [128, 4], F32).ap()
        self.cst = A("cst", [128, 4], F32).ap()
        self.st4 = A("st4", [128, 24], F32).ap()
        self.x1m = A("x1m", [128, 4 * E], F32).ap()
        self.scrA = A("scrA", [128, E], F32).ap()
        self.scrB = A("scrB", [128, E], F32).ap()
        self.sphi = A("sphi", [128, E], BF16).ap()
        self.splo = A("splo", [128, E], BF16).ap()
        self.bstage = A("bstage", [128, E], BF16).ap()
        self.i32t = A("i32t", [128, E], I32).ap()
        self.qstg = A("qstg", [128, 512], F32).ap()
        self.vstg = A("vstg", [128, 512], F32).ap()
        AR = 67584
        arena_h = A("arena", [128, AR], U8)
        base = nc.lookup_mloc(arena_h).addr

        def at(name, shape, dtype, off):
            h = nc.virtual_sbuf_tensor(name, shape, dtype)
            nc._tensor(h.name, h.shape, h.dtype, type="SB")
            mloc = nc.lookup_mloc(h)
            mloc.allocated = True
            mloc.addr = base + off
            per_part = int(np.prod(shape[1:])) * mybir.dt.size(dtype)
            assert off % 32 == 0 and off + per_part <= AR, name
            return h.ap()

        self.iosc = at("iosc", [128, 512], F32, 0)
        self.wqbh = at("wqbh", [128, 2 * 2048], BF16, 0)
        self.wqbl = at("wqbl", [128, 2 * 2048], BF16, 8192)
        self.wkvb = at("wkvb", [128, 4 * 2048], BF16, 16384)
        self.lnTh = at("lnTh", [128, NEC * 512], BF16, 32768)
        self.lnTl = at("lnTl", [128, NEC * 512], BF16, 49152)
        self.qhist = at("qhist", [128, 512], BF16, 65536)
        self.qlost = at("qlost", [128, 512], BF16, 66560)
        self.e32 = at("e32", [128, 4 * 512], F32, 0)
        self.ehi = at("ehi", [128, 4 * 512], BF16, 8192)
        self.elo = at("elo", [128, 4 * 512], BF16, 12288)
        self.qrh = at("qrh", [128, 2 * 512], BF16, 16384)
        self.qrl = at("qrl", [128, 2 * 512], BF16, 18432)
        self.outT = at("outT", [128, H * 512], F32, 20480)
        self.Rsb = at("Rsb", [128, 512], F32, 53248)
        self.RTsb = at("RTsb", [128, 64], F32, 55296)
        self.wfb = at("wfb", [128, 2 * 8192], BF16, 0)
        self.xqTt = at("xqTt", [128, 4 * 2048], BF16, 32768)

    def alloc_psum(self):
        P = self.nc.alloc_psum_tensor
        self.scores4 = P("scores4", [128, 2048], F32).ap()
        self.pO1 = P("pO1", [128, 512], F32).ap()
        self.pS = P("pS", [128, 512], F32).ap()
        self.pb = {"pb1": P("pb1", [128, 1024], BF16).ap(),
                   "pb2": P("pb2", [128, 1024], BF16).ap()}
        self.bank_w = {}
        self.bank_r = {}
        self._pbn = ["pb1", "pb2"]

    def sbank(self, i):
        return self.scores4[:, i * 512:(i + 1) * 512]

    def bf16_transpose_batch(self, items, dst_fn, extra_waits=()):
        toks = []
        for b0 in range(0, len(items), 8):
            chunk = items[b0:b0 + 8]
            pbn = self._pbn[0]
            self._pbn.reverse()
            pbap = self.pb[pbn]
            t = None
            n = len(chunk)
            for i, srcap in enumerate(chunk):
                w = (list(extra_waits) + [self.bank_r.get(pbn)]) if i == 0 else []
                t = self.tt(lambda e, pbap=pbap, i=i, srcap=srcap, n=n:
                    e.matmul(pbap[:, i * 128:(i + 1) * 128], srcap,
                             self.identb[:, :], start=(i == 0),
                             stop=(i == n - 1), is_transpose=True),
                    waits=w, inc="t" if i == n - 1 else None)
            self.bank_w[pbn] = t
            tv = dst_fn(b0 // 8, pbap, t)
            self.bank_r[pbn] = tv
            toks.append(tv)
        return toks

    def setup(self):
        s = self
        s.tv(lambda e: e.memset(s.cst[:, 0:1], 0.5), inc=None)
        s.tv(lambda e: e.memset(s.cst[:, 1:2], -0.5), inc=None)
        s.tv(lambda e: e.memset(s.cst[:, 2:3], EPS), inc=None)
        self.cst_tok = s.tv(lambda e: e.memset(s.cst[:, 3:4], SHIFT))
        tg = s.emit("g", lambda e: e.iota(s.iosc[:, 0:128], pattern=[[0, 128]],
                                          base=0, channel_multiplier=1,
                                          allow_small_or_imprecise_dtypes=True),
                    inc="g")
        tg = s.emit("g", lambda e: e.iota(s.iosc[:, 128:256], pattern=[[1, 128]],
                                          base=0, channel_multiplier=0,
                                          allow_small_or_imprecise_dtypes=True),
                    inc="g")
        iop = s.iosc[:, 0:128]
        ioj = s.iosc[:, 128:256]
        s.tv(lambda e: e.tensor_tensor(s.identf[:, :], ioj, iop, op=ALU.is_equal),
             waits=[tg], inc=None)
        s.tv(lambda e: e.tensor_copy(s.identb[:, :], s.identf[:, :]), inc=None)
        s.tv(lambda e: e.memset(s.onesb[:, :], 1.0), inc=None)
        for h in range(H):
            s.tv(lambda e, h=h: e.tensor_scalar(s.ohb[:, h * 16:(h + 1) * 16],
                                                ioj[:, 0:16], float(h), None,
                                                op0=ALU.is_equal), inc=None)
        tv_id = s.tv(lambda e: e.tensor_copy(s.iosc[:, 256:257], s.identf[:, 0:1]))
        tv_prev = tv_id
        for i in range(4):
            tgi = s.emit("g", lambda e, i=i: e.iota(
                s.iosc[:, 0:512], pattern=[[1, 512]], base=-128 * i,
                channel_multiplier=-1, allow_small_or_imprecise_dtypes=True),
                waits=[tv_prev], inc="g")
            tv_prev = s.tv(lambda e, i=i: e.tensor_scalar(
                s.mk[:, i * 512:(i + 1) * 512], s.iosc[:, 0:512], 0.0, None,
                op0=ALU.is_ge), waits=[tgi])
        tl1 = s.load(s.iosc[0:16, 0:128], s.bq_h[:, :], "msc", waits=[tv_prev])
        tl2 = s.load(s.iosc[0:4, 128:256], s.bk_h[:, :], "msc")
        tl3 = s.load(s.iosc[0:4, 256:384], s.bv_h[:, :], "msc")
        s.tt(lambda e: e.matmul(s.pO1[:, 0:16], s.iosc[0:16, 0:128],
                                s.identf[0:16, 0:16], start=True, stop=False,
                                is_transpose=True),
             waits=[tl1, tl2, tl3, tv_id], inc=None)
        s.tt(lambda e: e.matmul(s.pO1[:, 16:20], s.iosc[0:4, 128:256],
                                s.identf[0:4, 0:4], start=False, stop=False,
                                is_transpose=True), inc=None)
        ttk = s.tt(lambda e: e.matmul(s.pO1[:, 32:36], s.iosc[0:4, 256:384],
                                      s.identf[0:4, 0:4], start=False,
                                      stop=True, is_transpose=True))
        s.tv(lambda e: e.tensor_copy(s.bqT[:, :], s.pO1[:, 0:16]), waits=[ttk],
             inc=None)
        s.tv(lambda e: e.tensor_copy(s.bkT[:, :], s.pO1[:, 16:20]), inc=None)
        tvb = s.tv(lambda e: e.tensor_copy(s.bvT[:, :], s.pO1[:, 32:36]))
        s.bank_r["pO1"] = tvb
        s.ta(lambda e: e.preload_act_table(AF.Sqrt), inc=None)

        tz = s.tv(lambda e: e.memset(s.st4[:, 0:16], 0.0))
        wslot = [None, None]
        ta = None
        for oc in range(16):
            sl = oc % 2
            xap = s.x1m[:, sl * E:(sl + 1) * E]
            tl = s.load(xap, s.wf_h[oc * 128:(oc + 1) * 128, :], "x1m%d" % sl,
                        waits=[wslot[sl]])
            ta = s.ta(lambda e, xap=xap, oc=oc: e.activation(
                s.scrA[:, :], xap, AF.Abs, accum_out=s.st4[:, oc:oc + 1]),
                waits=[tl, tz])
            wslot[sl] = ta
        s.tv(lambda e: e.tensor_copy(s.sphi[:, 0:16], s.st4[:, 0:16]),
             waits=[ta], inc=None)
        s.vgap()
        s.tv(lambda e: e.tensor_copy(s.iosc[:, 384:400], s.sphi[:, 0:16]),
             inc=None)
        s.vgap()
        s.tv(lambda e: e.tensor_tensor(s.iosc[:, 400:416], s.st4[:, 0:16],
                                       s.iosc[:, 384:400], op=ALU.subtract),
             inc=None)
        s.vgap()
        tlo = s.tv(lambda e: e.tensor_copy(s.splo[:, 0:16], s.iosc[:, 400:416]))
        s.tt(lambda e: e.matmul(s.pO1[:, 0:16], s.onesb[:, :], s.sphi[:, 0:16],
                                start=True, stop=False),
             waits=[tlo, s.bank_r["pO1"]], inc=None)
        ttk = s.tt(lambda e: e.matmul(s.pO1[:, 0:16], s.onesb[:, :],
                                      s.splo[:, 0:16], start=False, stop=True))
        s.bank_w["pO1"] = ttk
        tv1 = s.tv(lambda e: e.pool_avg(s.st4[:, 0:1], s.pO1[:, 0:16]),
                   waits=[ttk])
        s.bank_r["pO1"] = tv1
        ta1 = s.ta(lambda e: e.activation(s.st4[:, 1:2], s.st4[:, 0:1],
                                          AF.Identity), waits=[tv1])
        s.tv(lambda e: e.tensor_scalar(s.wsc[:, 0:1], s.st4[:, 1:2],
                                       float(16.0 / (E * E)), 1e-6,
                                       op0=ALU.mult, op1=ALU.max),
             waits=[ta1], inc=None)
        s.vgap()
        s.tv(lambda e: e.reciprocal(s.wsc[:, 1:2], s.wsc[:, 0:1]), inc=None)
        tv_ws = s.tv(lambda e: e.tensor_scalar(s.wsc[:, 2:3], s.wsc[:, 0:1],
                                               float(1.0 / 127.0), None,
                                               op0=ALU.mult))
        if s.debug:
            s.tv(lambda e: e.tensor_copy(s.iosc[:, 0:16], s.st4[:, 0:16]),
                 inc=None)
            s.tv(lambda e: e.tensor_copy(s.iosc[:, 16:32], s.pO1[:, 0:16]),
                 inc=None)
            s.tv(lambda e: e.tensor_copy(s.iosc[:, 32:36], s.wsc[:, 0:4]),
                 inc=None)
            s.tv(lambda e: e.tensor_copy(s.iosc[:, 36:52], s.sphi[:, 0:16]),
                 inc=None)
            tdg = s.tv(lambda e: e.tensor_copy(s.iosc[:, 52:68], s.splo[:, 0:16]))
            s.store(s.dbg_d[:, 0:68], s.iosc[:, 0:68], "st_dbg", waits=[tdg])

        wslot = [None, None]
        st_done = None
        for oc in range(16):
            sl = oc % 2
            xap = s.x1m[:, sl * E:(sl + 1) * E]
            tl = s.load(xap, s.wf_h[oc * 128:(oc + 1) * 128, :], "x1m%d" % sl,
                        waits=[wslot[sl], tv_ws])
            tvv = s.tv(lambda e, xap=xap: e.tensor_scalar(
                s.scrB[:, :], xap, s.wsc[:, 1:2], None, op0=ALU.mult),
                waits=[tl, s.scr_tok["B"]])
            wslot[sl] = tvv
            ta = s.ta(lambda e: e.activation(s.i32t[:, :], s.scrB[:, :],
                                             AF.Identity),
                      waits=[tvv, s.i32_tok])
            s.i32_tok = s.tv(lambda e: e.tensor_copy(s.bstage[:, :],
                                                     s.i32t[:, :]),
                             waits=[ta, s.bstage_tok])
            tvv = s.tv(lambda e: e.tensor_tensor(s.scrA[:, :], s.scrB[:, :],
                                                 s.bstage[:, :],
                                                 op=ALU.subtract))
            s.ta(lambda e: e.activation(s.sphi[:, :], s.scrA[:, :], AF.Sign,
                                        bias=s.cst[:, 0:1]),
                 waits=[tvv, s.sphi_tok], inc=None)
            ta = s.ta(lambda e: e.activation(s.splo[:, :], s.scrA[:, :], AF.Sign,
                                             bias=s.cst[:, 1:2]), waits=[s.splo_tok])
            tvv = s.tv(lambda e: e.tensor_tensor(s.scrB[:, :], s.sphi[:, :],
                                                 s.splo[:, :], op=ALU.add),
                       waits=[ta])
            s.scr_tok["B"] = tvv
            tvv = s.tv(lambda e: e.tensor_scalar(s.scrA[:, :], s.scrB[:, :], 0.5,
                                                 None, op0=ALU.mult))
            s.scr_tok["A"] = tvv
            tvv = s.tv(lambda e: e.tensor_tensor(s.scrB[:, :], s.bstage[:, :],
                                                 s.scrA[:, :], op=ALU.add))
            tvq = s.tv(lambda e: e.tensor_scalar(s.bstage[:, :], s.scrB[:, :],
                                                 -1.0, 1.0, op0=ALU.max,
                                                 op1=ALU.min))
            items = [s.bstage[:, ic * 128:(ic + 1) * 128] for ic in range(16)]

            def dst_fn(bi, pbap, ttok, prev=st_done):
                return s.tv(lambda e, bi=bi, pbap=pbap: e.tensor_copy(
                    s.sphi[:, bi * 1024:(bi + 1) * 1024], pbap[:, 0:1024]),
                    waits=[ttok, prev])

            toks = s.bf16_transpose_batch(items, dst_fn, extra_waits=[tvq])
            s.bstage_tok = toks[-1]
            s.scr_tok["A"] = tvq
            s.scr_tok["B"] = tvq
            og, osub = oc // 4, oc % 4
            dst = s.wfT_d.rearrange("p (og ic osub ol) -> p og ic osub ol",
                                    og=4, ic=16, osub=4)[:, og, :, osub, :]
            src = s.sphi.rearrange("p (ic ol) -> p ic ol", ic=16)
            st_done = s.store(dst, src, "st_wf", waits=toks)
            s.sphi_tok = st_done

        jobs = [(s.wq_h, h, s.wqh_d, s.wql_d, h) for h in range(H)]
        jobs += [(s.wk_h, g, s.wkh_d, s.wkl_d, g) for g in range(G)]
        jobs += [(s.wv_h, g, s.wvh_d, s.wvl_d, g) for g in range(G)]
        sp_st = [st_done, st_done]
        for j, (src_h, idx, dh, dl, col) in enumerate(jobs):
            sl = j % 2
            xap = s.x1m[:, sl * E:(sl + 1) * E]
            src = src_h.rearrange("h (ec p) d -> h p ec d", p=128)[idx]
            tl = s.load(xap.rearrange("p (ec d) -> p ec d", ec=16), src,
                        "x1m%d" % sl, waits=[wslot[sl]])
            thi = s.tv(lambda e, xap=xap: e.tensor_copy(s.sphi[:, :], xap),
                       waits=[tl, sp_st[0], s.sphi_tok])
            tlo = s.tv(lambda e, xap=xap: e.tensor_tensor(
                s.splo[:, :], xap, s.sphi[:, :], op=ALU.subtract),
                waits=[sp_st[1], s.splo_tok])
            s.sphi_tok = None
            s.splo_tok = None
            wslot[sl] = tlo
            sp_st[0] = s.store(dh[:, col * 2048:(col + 1) * 2048], s.sphi[:, :],
                               "st_jh", waits=[thi])
            sp_st[1] = s.store(dl[:, col * 2048:(col + 1) * 2048], s.splo[:, :],
                               "st_jl", waits=[tlo])
        self.setup_done = None
        self.setup_done_multi = [("st_wf", self.cnt["st_wf"]),
                                 ("st_jh", self.cnt["st_jh"]),
                                 ("st_jl", self.cnt["st_jl"]), sp_st[0]]
        self.sphi_tok = sp_st[0]
        self.splo_tok = sp_st[1]
        self.x1m_tok = [wslot[0], wslot[1], None, None]

    def p1(self, sq):
        s = self
        s.ta(lambda e: e.preload_act_table(AF.Sqrt), inc=None)
        bank_names = ["s0", "s1", "s2", "s3"]
        lnT_read = None
        cache_tok = None
        ttk = None
        for mt in range(NMT):
            lnT_evacs = []
            for tl_i in range(4):
                t = mt * 4 + tl_i
                xap = s.x1m[:, tl_i * E:(tl_i + 1) * E]
                tld = s.load(xap, s.x_h[sq, t * 128:(t + 1) * 128, :],
                             "x1m%d" % tl_i, waits=[s.x1m_tok[tl_i]])
                par = (t % 2) * 8
                P = "st:%d" % par
                mean_ap, rstd_ap, tst = self.ln_stats_waitload(xap, par, tld)
                tln = s.tv(lambda e, xap=xap, m=mean_ap, r=rstd_ap:
                           e.tensor_scalar(s.scrA[:, :], xap, m, r,
                                           op0=ALU.subtract, op1=ALU.mult),
                           waits=[tst, s.scr_tok["A"]],
                           reads=("x1m:%d" % tl_i, P + "m", P + "v"),
                           writes=("scrA",))
                s.x1m_tok[tl_i] = tln
                thi = s.tv(lambda e: e.tensor_copy(s.sphi[:, :], s.scrA[:, :]),
                           waits=[s.sphi_tok], reads=("scrA",), writes=("sph",))
                tlo = s.tv(lambda e: e.tensor_tensor(
                    s.splo[:, :], s.scrA[:, :], s.sphi[:, :], op=ALU.subtract),
                    waits=[s.splo_tok], reads=("scrA", "sph"), writes=("spl",))
                s.scr_tok["A"] = tlo
                toks = None
                for (srcb, dstb, half) in ((s.sphi, s.lnTh, 0), (s.sphi, s.lnTh, 1),
                                           (s.splo, s.lnTl, 0), (s.splo, s.lnTl, 1)):
                    items = [srcb[:, (half * 8 + i) * 128:(half * 8 + i + 1) * 128]
                             for i in range(8)]
                    dst = dstb.rearrange("p (ec q) -> p ec q", ec=16)[
                        :, half * 8:half * 8 + 8, tl_i * 128:(tl_i + 1) * 128]

                    def dst_fn(bi, pbap, ttok, dst=dst, srcb=srcb, half=half):
                        key = "lnT:%d:%d" % (id(srcb) % 97, half)
                        return s.tv(lambda e, pbap=pbap, dst=dst: e.tensor_copy(
                            dst, pbap.rearrange("p (i q) -> p i q", i=8)),
                            waits=[ttok, lnT_read, s.arena_free],
                            reads=(), writes=(key,))

                    toks = s.bf16_transpose_batch(items, dst_fn,
                                                  extra_waits=[tlo])
                    lnT_evacs += toks
                s.sphi_tok = toks[-1]
                s.splo_tok = toks[-1]
            lnT_ready = lnT_evacs[-1]

            units = [("q", h) for h in range(H)] + \
                    [("k", g) for g in range(G)] + [("v", g) for g in range(G)]
            ld_toks = {}

            def emit_unit_load(kind, idx):
                if kind == "q":
                    sl = idx % 2
                    ld_toks[(kind, idx)] = (
                        s.load(s.wqbh[:, sl * 2048:(sl + 1) * 2048],
                               s.wqh_d[:, idx * 2048:(idx + 1) * 2048],
                               "wqs%d" % sl,
                               waits=[s.wq_slot.get(sl), s.arena_free]
                                     + s.setup_done_multi),
                        s.load(s.wqbl[:, sl * 2048:(sl + 1) * 2048],
                               s.wql_d[:, idx * 2048:(idx + 1) * 2048],
                               "wqs%d" % sl))
                else:
                    sl = idx % 2
                    hi_d = s.wkh_d if kind == "k" else s.wvh_d
                    lo_d = s.wkl_d if kind == "k" else s.wvl_d
                    ld_toks[(kind, idx)] = (
                        s.load(s.wkvb[:, (sl * 2) * 2048:(sl * 2 + 1) * 2048],
                               hi_d[:, idx * 2048:(idx + 1) * 2048],
                               "kvs%d" % sl,
                               waits=[s.kv_slot.get(sl), s.arena_free]
                                     + s.setup_done_multi),
                        s.load(s.wkvb[:, (sl * 2 + 1) * 2048:(sl * 2 + 2) * 2048],
                               lo_d[:, idx * 2048:(idx + 1) * 2048],
                               "kvs%d" % sl))

            emit_unit_load(*units[0])
            emit_unit_load(*units[1])
            qst_free = None
            qhist_free = ("stq_h", s.cnt["stq_h"])
            qlost_free = ("stq_l", s.cnt["stq_l"])
            vstg_free = None
            for ui, (kind, idx) in enumerate(units):
                bn = bank_names[ui % 4]
                bap = s.sbank(ui % 4)
                lt = ld_toks[(kind, idx)]
                if kind == "q":
                    whi = s.wqbh[:, (idx % 2) * 2048:((idx % 2) + 1) * 2048]
                    wlo = s.wqbl[:, (idx % 2) * 2048:((idx % 2) + 1) * 2048]
                else:
                    sl = (idx % 2) * 2
                    whi = s.wkvb[:, sl * 2048:(sl + 1) * 2048]
                    wlo = s.wkvb[:, (sl + 1) * 2048:(sl + 2) * 2048]
                first = True
                for ec in range(NEC):
                    for pi, (wap, rap) in enumerate(
                            ((whi, s.lnTh), (whi, s.lnTl), (wlo, s.lnTh))):
                        w = [lt[0], lt[1], lnT_ready,
                             s.bank_r.get(bn)] if first else []
                        last = (ec == NEC - 1) and (pi == 2)
                        ttk = s.tt(lambda e, bap=bap, wap=wap, rap=rap, ec=ec,
                                   first=first, last=last: e.matmul(
                                       bap, wap[:, ec * 128:(ec + 1) * 128],
                                       rap[:, ec * 512:(ec + 1) * 512],
                                       start=first, stop=last),
                                   waits=w, inc="t" if last else None)
                        first = False
                s.bank_w[bn] = ttk
                if kind == "q":
                    s.wq_slot[idx % 2] = ttk
                else:
                    s.kv_slot[idx % 2] = ttk
                if ui + 2 < len(units):
                    emit_unit_load(*units[ui + 2])
                if kind == "q":
                    tq = s.tv(lambda e, bap=bap, idx=idx: e.tensor_scalar(
                        s.qstg[:, :], bap, s.bqT[:, idx:idx + 1], None,
                        op0=ALU.add), waits=[ttk, qst_free],
                        reads=("bq",), writes=("qstg",))
                    s.bank_r[bn] = tq
                    thi = s.tv(lambda e: e.tensor_copy(s.qhist[:, :],
                                                       s.qstg[:, :]),
                               waits=[qhist_free, s.arena_free],
                               reads=("qstg",), writes=("qh",))
                    tlo = s.tv(lambda e: e.tensor_tensor(
                        s.qlost[:, :], s.qstg[:, :], s.qhist[:, :],
                        op=ALU.subtract), waits=[qlost_free],
                        reads=("qstg", "qh"), writes=("ql",))
                    qst_free = tlo
                    base = idx * 2048 + mt * 512
                    qhist_free = s.store(s.qh_d[sq, :, base:base + 512],
                                         s.qhist[:, :], "stq_h", waits=[thi])
                    qlost_free = s.store(s.ql_d[sq, :, base:base + 512],
                                         s.qlost[:, :], "stq_l", waits=[tlo])
                elif kind == "k":
                    tq = s.tv(lambda e, bap=bap, idx=idx: e.tensor_scalar(
                        s.qstg[:, :], bap, s.bkT[:, idx:idx + 1], None,
                        op0=ALU.add), waits=[ttk, qst_free, s.kv_read],
                        reads=("bk",), writes=("qstg",))
                    s.bank_r[bn] = tq
                    dh = s.kch[:, idx * 2048 + mt * 512:idx * 2048 + (mt + 1) * 512]
                    dl = s.kcl[:, idx * 2048 + mt * 512:idx * 2048 + (mt + 1) * 512]
                    s.tv(lambda e, dh=dh: e.tensor_copy(dh, s.qstg[:, :]),
                         inc=None, reads=("qstg",), writes=("kch",))
                    cache_tok = s.tv(lambda e, dh=dh, dl=dl: e.tensor_tensor(
                        dl, s.qstg[:, :], dh, op=ALU.subtract),
                        reads=("qstg", "kch"), writes=("kcl",))
                    qst_free = cache_tok
                else:
                    tq = s.tv(lambda e, bap=bap, idx=idx: e.tensor_scalar(
                        s.vstg[:, :], bap, s.bvT[:, idx:idx + 1], None,
                        op0=ALU.add),
                        waits=[ttk, vstg_free, s.kv_read,
                               getattr(s, "vstg_g", None)],
                        reads=("bv",), writes=("vstg",))
                    s.bank_r[bn] = tq
                    ttv = None
                    for sub in range(4):
                        w = [tq, s.bank_r.get("pO1")] if sub == 0 else []
                        ttv = s.tt(lambda e, sub=sub: e.matmul(
                            s.pO1[:, sub * 128:(sub + 1) * 128],
                            s.vstg[:, sub * 128:(sub + 1) * 128],
                            s.identf[:, :], start=(sub == 0), stop=(sub == 3),
                            is_transpose=True),
                            waits=w, inc="t" if sub == 3 else None)
                    s.bank_w["pO1"] = ttv
                    vstg_free = ttv
                    base = (idx * 16 + mt * 4) * 128
                    s.tv(lambda e, base=base: e.tensor_copy(
                        s.vch[:, base:base + 512], s.pO1[:, 0:512]),
                        waits=[ttv], inc=None, reads=(), writes=("vch",))
                    cache_tok = s.tv(lambda e, base=base: e.tensor_tensor(
                        s.vcl[:, base:base + 512], s.pO1[:, 0:512],
                        s.vch[:, base:base + 512], op=ALU.subtract),
                        reads=("vch",), writes=("vcl",))
                    s.bank_r["pO1"] = cache_tok
            lnT_read = ttk
        self.arena_free = ttk
        return cache_tok

    def ln_stats_waitload(self, xap, par, tld):
        s = self
        P = "st:%d" % par
        cc = [s.st4[:, par + i:par + i + 1] for i in range(6)]
        xk = "x1m:*"
        tz = s.tv(lambda e, par=par: e.memset(s.st4[:, par:par + 2], 0.0),
                  waits=[s.par_tok.get(par)], reads=(), writes=(P + "a",))
        s.ta(lambda e, xap=xap, c=cc[0]: e.activation(
            s.scrB[:, :], xap, AF.Identity, accum_out=c),
            waits=[tld, tz, s.scr_tok["B"]], inc=None,
            reads=(xk,), writes=("scrB", P + "a"))
        ta = s.ta(lambda e, xap=xap, c=cc[1]: e.activation(
            s.scrB[:, :], xap, AF.Square, accum_out=c),
            reads=(xk,), writes=("scrB", P + "a"))
        s.par_tok[par] = ta
        s.scr_tok["B"] = ta
        s.tv(lambda e, c=cc: e.tensor_scalar(c[2], c[0], INV_E, None,
                                             op0=ALU.mult), waits=[ta],
             inc=None, reads=(P + "a",), writes=(P + "m",))
        s.tv(lambda e, c=cc: e.tensor_scalar(c[3], c[1], INV_E, None,
                                             op0=ALU.mult), inc=None,
             reads=(P + "a",), writes=(P + "q",))
        s.tv(lambda e, c=cc: e.tensor_tensor(c[4], c[2], c[2], op=ALU.mult),
             inc=None, reads=(P + "m",), writes=(P + "v",))
        tvv = s.tv(lambda e, c=cc: e.tensor_tensor(c[4], c[3], c[4],
                                                   op=ALU.subtract),
                   reads=(P + "q", P + "v"), writes=(P + "v",))
        ta = s.ta(lambda e, c=cc: e.activation(c[5], c[4], AF.Sqrt,
                                               bias=s.cst[:, 2:3]),
                  waits=[tvv], reads=(P + "v", "cst"), writes=(P + "s",))
        tst = s.tv(lambda e, c=cc: e.reciprocal(c[4], c[5]), waits=[ta],
                   reads=(P + "s",), writes=(P + "v",))
        return cc[2], cc[4], tst

    def p2(self, sq, p1_tok):
        s = self
        s.ta(lambda e: e.preload_act_table(AF.Exp), inc=None)
        e32_free = {}
        e_used = {}
        exp_done = {}
        last_store = None
        for mt in range(NMT):
            C = 4 * (mt + 1)
            x1_ld = []
            for tl_i in range(4):
                x1_ld.append(s.load(
                    s.x1m[:, tl_i * E:(tl_i + 1) * E],
                    s.x_h[sq, (mt * 4 + tl_i) * 128:(mt * 4 + tl_i + 1) * 128, :],
                    "x1m%d" % tl_i, waits=[s.x1m_tok[tl_i]]))
            q_ld = {}

            def q_load(h, mt=mt):
                sl = h % 2
                base = h * 2048 + mt * 512
                q_ld[h] = (
                    s.load(s.qrh[:, sl * 512:(sl + 1) * 512],
                           s.qh_d[sq, :, base:base + 512], "qs%d" % sl,
                           waits=[s.qr_slot.get(sl), p1_tok, s.arena_free,
                                  ("stq_h", s.cnt["stq_h"]),
                                  ("stq_l", s.cnt["stq_l"])]),
                    s.load(s.qrl[:, sl * 512:(sl + 1) * 512],
                           s.ql_d[sq, :, base:base + 512], "qs%d" % sl))

            q_load(0)
            q_load(1)
            sums_started = [False]
            last_sums = None
            out_evacs = []
            for h in range(H):
                g = h // 4
                qh = s.qrh[:, (h % 2) * 512:((h % 2) + 1) * 512]
                ql = s.qrl[:, (h % 2) * 512:((h % 2) + 1) * 512]
                obn = "pO1" if h % 2 == 0 else "s3"
                obap = s.pO1 if h % 2 == 0 else s.sbank(3)
                split_args = {}

                def emit_scores(c, h=h, g=g, qh=qh, ql=ql, mt=mt):
                    bn = c % 3
                    band_i = c - 4 * mt
                    q0 = 128 * band_i if band_i > 0 else 0
                    bap = s.sbank(bn)[:, q0:512]
                    kh = s.kch[:, g * 2048 + c * 128:g * 2048 + (c + 1) * 128]
                    kl = s.kcl[:, g * 2048 + c * 128:g * 2048 + (c + 1) * 128]
                    w0 = [q_ld[h][0], q_ld[h][1], exp_done.get(bn),
                          s.bank_r.get("s%d" % bn), p1_tok]
                    tk = None
                    for pi, (wap, rap) in enumerate(
                            ((kh, qh[:, q0:512]), (kh, ql[:, q0:512]),
                             (kl, qh[:, q0:512]))):
                        tk = s.tt(lambda e, bap=bap, wap=wap, rap=rap, pi=pi:
                                  e.matmul(bap, wap, rap, start=(pi == 0),
                                           stop=(pi == 2)),
                                  waits=w0 if pi == 0 else [],
                                  inc="t" if pi == 2 else None)
                    s.bank_w["s%d" % bn] = tk
                    esl = c % 4
                    ta = s.ta(lambda e, bap=bap, esl=esl, q0=q0: e.activation(
                        s.e32[:, esl * 512 + q0:(esl + 1) * 512], bap, AF.Exp,
                        bias=s.cst[:, 3:4], scale=SCALE),
                        waits=[tk, e32_free.get(esl), s.arena_free],
                        reads=("cst",), writes=("e32:%d" % esl,))
                    exp_done[bn] = ta
                    e32ap = s.e32[:, esl * 512 + q0:(esl + 1) * 512]
                    ehiap = s.ehi[:, esl * 512 + q0:(esl + 1) * 512]
                    eloap = s.elo[:, esl * 512 + q0:(esl + 1) * 512]
                    wsp = [ta, e_used.get(esl), s.arena_free]
                    split_args[c] = (esl, band_i, q0, e32ap, ehiap, eloap, wsp)

                def emit_pv(c, h=h, g=g, obap=obap, obn=obn, C=C):
                    esl, band_i, q0, e32ap, ehiap, eloap, wsp = \
                        split_args.pop(c)
                    ek = "e32:%d" % esl
                    hk = "eh:%d" % esl
                    lk = "el:%d" % esl
                    W = 512 - q0
                    if band_i >= 0:
                        mki = s.mk[:, band_i * 512 + q0:(band_i + 1) * 512]
                        s.tv(lambda e, a=ehiap, b=e32ap, m=mki:
                             e.tensor_tensor(a, b, m, op=ALU.mult),
                             waits=wsp, inc=None, reads=(ek, "mk"),
                             writes=(hk,))
                        s.tv(lambda e, b=e32ap, hh=ehiap, W=W:
                             e.tensor_tensor(s.qstg[:, 0:W], b, hh,
                                             op=ALU.subtract),
                             inc=None, reads=(ek, hk), writes=("qstg",))
                        tvv = s.tv(lambda e, a=eloap, m=mki, W=W:
                                   e.tensor_tensor(a, s.qstg[:, 0:W], m,
                                                   op=ALU.mult),
                                   reads=("qstg", "mk"), writes=(lk,))
                    else:
                        s.tv(lambda e, a=ehiap, b=e32ap: e.tensor_copy(a, b),
                             waits=wsp, inc=None, reads=(ek,), writes=(hk,))
                        tvv = s.tv(lambda e, a=eloap, b=e32ap, hh=ehiap:
                                   e.tensor_tensor(a, b, hh,
                                                   op=ALU.subtract),
                                   reads=(ek, hk), writes=(lk,))
                    e32_free[esl] = tvv
                    vh = s.vch[:, (g * 16 + c) * 128:(g * 16 + c + 1) * 128]
                    vl = s.vcl[:, (g * 16 + c) * 128:(g * 16 + c + 1) * 128]
                    ob = obap[:, q0:512]
                    w0 = [e32_free[esl]]
                    if c == 0:
                        w0.append(s.bank_r.get(obn))
                    for pi, (wap, rap) in enumerate(((vh, ehiap), (vl, ehiap),
                                                     (vh, eloap))):
                        s.tt(lambda e, ob=ob, wap=wap, rap=rap, pi=pi, c=c:
                             e.matmul(ob, wap, rap,
                                      start=(c == 0 and pi == 0),
                                      stop=(c == C - 1 and pi == 2)),
                             waits=w0 if pi == 0 else [], inc=None)
                    oh = s.ohb[:, h * 16:(h + 1) * 16]
                    st0 = not sums_started[0]
                    wS = [s.bank_r.get("pS")] if st0 else []
                    s.tt(lambda e, oh=oh, rap=ehiap, st0=st0, q0=q0: e.matmul(
                        s.pS[0:16, q0:512], oh, rap, start=st0, stop=False),
                        waits=wS, inc=None)
                    laststop = (h == H - 1 and c == C - 1)
                    tk = s.tt(lambda e, oh=oh, rap=eloap, ls=laststop, q0=q0:
                              e.matmul(s.pS[0:16, q0:512], oh, rap,
                                       start=False, stop=ls))
                    sums_started[0] = True
                    e_used[esl] = tk
                    return tk

                last_pv = None
                LAG = 2
                for c in range(C):
                    emit_scores(c)
                    if c >= LAG:
                        last_pv = emit_pv(c - LAG)
                for c in range(max(0, C - LAG), C):
                    last_pv = emit_pv(c)
                last_sums = last_pv
                s.bank_w[obn] = last_pv
                tv = s.ta(lambda e, obap=obap, h=h: e.activation(
                    s.outT[:, h * 512:(h + 1) * 512], obap[:, 0:512],
                    AF.Identity),
                    waits=[last_pv, s.arena_free],
                    reads=(), writes=("outT:%d" % h,))
                s.bank_r[obn] = tv
                s.qr_slot[h % 2] = tv
                out_evacs.append(tv)
                if h + 2 < H:
                    q_load(h + 2)
            s.bank_w["pS"] = last_sums

            trc = s.tv(lambda e: e.reciprocal(s.Rsb[0:16, :], s.pS[0:16, :]),
                       waits=[last_sums, s.arena_free],
                       reads=(), writes=("Rsb",))
            s.bank_r["pS"] = trc
            ttk = None
            for sub in range(4):
                w = [trc, exp_done.get(2), s.bank_r.get("s2")] if sub == 0 else []
                ttk = s.tt(lambda e, sub=sub: e.matmul(
                    s.sbank(2)[:, sub * 16:(sub + 1) * 16],
                    s.Rsb[0:16, sub * 128:(sub + 1) * 128],
                    s.identf[0:16, 0:16], start=(sub == 0), stop=(sub == 3),
                    is_transpose=True),
                    waits=w, inc="t" if sub == 3 else None)
            s.bank_w["s2"] = ttk
            s.tv(lambda e: e.tensor_copy(s.RTsb[:, :], s.sbank(2)[:, 0:64]),
                 waits=[ttk], inc=None, reads=(), writes=("RT",))
            trt = s.tv(lambda e: e.memset(s.st4[:, 22:23], 0.0),
                       reads=("RT",), writes=("st:22",))
            s.bank_r["s2"] = trt
            addtok = None
            for h in range(H):
                bn = "s0" if h % 2 == 0 else "s1"
                bap = s.sbank(0) if h % 2 == 0 else s.sbank(1)
                ttk = None
                for sub in range(4):
                    w = [out_evacs[-1], s.bank_r.get(bn),
                         exp_done.get(0), exp_done.get(1)] if sub == 0 else []
                    ttk = s.tt(lambda e, bap=bap, h=h, sub=sub: e.matmul(
                        bap[:, sub * 128:(sub + 1) * 128],
                        s.outT[:, h * 512 + sub * 128:h * 512 + (sub + 1) * 128],
                        s.identf[:, :], start=(sub == 0), stop=(sub == 3),
                        is_transpose=True),
                        waits=w, inc="t" if sub == 3 else None)
                s.bank_w[bn] = ttk
                for sub in range(4):
                    w = [ttk, trt, x1_ld[sub]]
                    s.tv(lambda e, bap=bap, h=h, sub=sub: e.tensor_scalar(
                        s.qstg[:, sub * 128:(sub + 1) * 128],
                        bap[:, sub * 128:(sub + 1) * 128],
                        s.RTsb[:, sub * 16 + h:sub * 16 + h + 1], None,
                        op0=ALU.mult), waits=w, inc=None,
                        reads=("RT",), writes=("qstg:%d" % sub,))
                for sub in range(4):
                    dstx = s.x1m[:, sub * E + h * 128:sub * E + (h + 1) * 128]
                    addtok = s.tv(lambda e, dstx=dstx, sub=sub: e.tensor_tensor(
                        dstx, dstx, s.qstg[:, sub * 128:(sub + 1) * 128],
                        op=ALU.add),
                        reads=("qstg:%d" % sub, "x1m:%d" % sub),
                        writes=("x1m:%d" % sub,))
                s.bank_r[bn] = addtok
            for tl_i in range(4):
                last_store = s.store(
                    s.x1_d[sq, mt * 4 + tl_i], s.x1m[:, tl_i * E:(tl_i + 1) * E],
                    "stx", waits=[addtok])
            for tl_i in range(4):
                s.x1m_tok[tl_i] = last_store
        self.arena_free = ttk
        self.kv_read = last_sums
        return last_store

    def p3(self, sq, p2_tok):
        s = self
        s.ta(lambda e: e.preload_act_table(AF.Sqrt), inc=None)
        xqT_read = None
        last_out = None
        ttk = None
        for mt in range(NMT):
            quant_last = None
            quant_tile = {}
            wf_ld = {}
            mm_tok = {}
            ev_fin = {}
            ev_last = [None]

            def wf_load(og, mt=mt, wf_ld=wf_ld):
                sl = og % 2
                wf_ld[og] = s.load(
                    s.wfb[:, sl * 8192:(sl + 1) * 8192],
                    s.wfT_d[:, og * 8192:(og + 1) * 8192], "wfs%d" % sl,
                    waits=[s.wf_slot.get(sl), s.arena_free, p2_tok]
                          + s.setup_done_multi)

            bmap = {(1, 0): "pO1", (1, 1): "pS", (1, 2): "s0",
                    (1, 3): "s1", (2, 0): "pO1", (2, 1): "pS",
                    (3, 0): "s0", (3, 1): "s1", (3, 2): "pO1", (3, 3): "pS"}

            def bank_of(og, tl_i):
                bn = bmap.get((og, tl_i), "s%d" % tl_i)
                if bn == "pO1":
                    return bn, s.pO1[:, :]
                if bn == "pS":
                    return bn, s.pS[:, :]
                return bn, s.sbank(int(bn[1]))

            def emit_mm(og, tl_i, wf_ld=wf_ld, mm_tok=mm_tok,
                        quant_tile=quant_tile):
                sl = og % 2
                bn, bap = bank_of(og, tl_i)
                ttk = None
                for ic in range(16):
                    w = [wf_ld[og], quant_tile[tl_i], s.bank_r.get(bn)] \
                        if ic == 0 else []
                    ttk = s.tt(lambda e, bap=bap, tl_i=tl_i, ic=ic, sl=sl:
                               e.matmul(
                                   bap,
                                   s.xqTt[:, tl_i * 2048 + ic * 128:
                                          tl_i * 2048 + (ic + 1) * 128],
                                   s.wfb[:, sl * 8192 + ic * 512:
                                         sl * 8192 + (ic + 1) * 512],
                                   start=(ic == 0), stop=(ic == 15)),
                               waits=w, inc="t" if ic == 15 else None)
                s.bank_w[bn] = ttk
                s.wf_slot[sl] = ttk
                mm_tok[(og, tl_i)] = ttk

            def emit_ev(og, tl_i, mm_tok=mm_tok, ev_last=ev_last,
                        ev_tok=None):
                bn, bap = bank_of(og, tl_i)
                sct = s.st4[:, 16 + tl_i:17 + tl_i]
                tvv = s.tv(lambda e, bap=bap, sct=sct: e.tensor_scalar(
                    s.vstg[:, :], bap, sct, s.wsc[:, 2:3],
                    op0=ALU.mult, op1=ALU.mult),
                    waits=[mm_tok[(og, tl_i)], getattr(s, "vstg_g", None)],
                    reads=("sct:%d" % tl_i, "wsc"), writes=("vstg",))
                s.bank_r[bn] = tvv
                dstx = s.x1m[:, tl_i * E + og * 512:tl_i * E + (og + 1) * 512]
                ev_last[0] = s.tv(lambda e, dstx=dstx: e.tensor_tensor(
                    dstx, dstx, s.vstg[:, :], op=ALU.add),
                    reads=("vstg", "x1m:%d" % tl_i),
                    writes=("x1m:%d" % tl_i,))
                ev_fin[(og, tl_i)] = ev_last[0]

            sched = {
                0: (("mm", 0, 0), ("mm", 1, 0)),
                1: (("ev", 0, 0), ("ev", 1, 0), ("mm", 0, 1), ("mm", 1, 1)),
                2: (("ev", 0, 1), ("ev", 1, 1), ("mm", 0, 2), ("mm", 1, 2)),
                3: (("ev", 0, 2), ("ev", 1, 2), ("mm", 0, 3), ("wf", 2, 0),
                    ("mm", 1, 3), ("wf", 3, 0), ("ev", 0, 3), ("ev", 1, 3),
                    ("mm", 2, 0), ("mm", 2, 1), ("mm", 2, 2), ("ev", 2, 0),
                    ("mm", 2, 3), ("ev", 2, 1), ("mm", 3, 0), ("ev", 2, 2),
                    ("mm", 3, 1), ("ev", 2, 3), ("mm", 3, 2), ("ev", 3, 0),
                    ("mm", 3, 3), ("ev", 3, 1), ("ev", 3, 2), ("ev", 3, 3)),
            }
            wf_load(0)
            wf_load(1)
            for tl_i in range(4):
                t = mt * 4 + tl_i
                xap = s.x1m[:, tl_i * E:(tl_i + 1) * E]
                tld = s.load(xap, s.x1_d[sq, t], "x1m%d" % tl_i,
                             waits=[s.x1m_tok[tl_i], ("stx", s.cnt["stx"])])
                if s.debug and sq == 0 and t == 1:
                    tld = s.store(s.dbg3_d[:, :], xap, "st_dbg", waits=[tld])
                par = (t % 2) * 8
                P = "st:%d" % par
                sct = s.st4[:, 16 + tl_i:17 + tl_i]
                mean_ap, rstd_ap, tst = self.ln_stats_waitload(xap, par, tld)
                c6 = s.st4[:, par + 6:par + 7]
                c7 = s.st4[:, par + 7:par + 8]
                xk = "x1m:%d" % tl_i
                tln = s.tv(lambda e, xap=xap, m=mean_ap, r=rstd_ap:
                           e.tensor_scalar(s.scrA[:, :], xap, m, r,
                                           op0=ALU.subtract, op1=ALU.mult),
                           waits=[tst, s.scr_tok["A"]],
                           reads=(xk, P + "m", P + "v"), writes=("scrA",))
                cA = s.st4[:, par + 1:par + 2]
                cR = s.st4[:, par + 2:par + 3]
                s.tv(lambda e: e.tensor_scalar(s.scrB[:, :], s.scrA[:, :],
                                               -1.0, None, op0=ALU.mult),
                     waits=[tln, s.scr_tok["B"]], inc=None,
                     reads=("scrA",), writes=("scrB",))
                s.tv(lambda e: e.tensor_tensor(s.scrB[:, :], s.scrA[:, :],
                                               s.scrB[:, :], op=ALU.max),
                     inc=None, reads=("scrA", "scrB"), writes=("scrB",))
                for half in (1024, 512, 256, 128, 64):
                    s.tv(lambda e, half=half: e.tensor_tensor(
                        s.scrB[:, 0:half], s.scrB[:, 0:half],
                        s.scrB[:, half:2 * half], op=ALU.max), inc=None,
                        reads=("scrB",), writes=("scrB",))
                tp = s.tv(lambda e, c6=c6: e.pool_max(c6, s.scrB[:, 0:64]),
                          reads=("scrB",), writes=(P + "p",))
                ta = s.ta(lambda e, c6=c6, cA=cA: e.activation(
                    cA, c6, AF.Identity), waits=[tp],
                    reads=(P + "p",), writes=(P + "A",))
                cC = s.st4[:, par + 0:par + 1]
                s.tv(lambda e, cA=cA, cC=cC: e.tensor_scalar(
                    cC, cA, 1e-8, None, op0=ALU.max), waits=[ta], inc=None,
                    reads=(P + "A",), writes=(P + "C",))
                s.tv(lambda e, cC=cC, sct=sct: e.tensor_copy(sct, cC),
                     waits=[getattr(s, "dbg_tok", None)], inc=None,
                     reads=(P + "C",), writes=("sct:%d" % tl_i,))
                s.tv(lambda e, cC=cC, cR=cR: e.reciprocal(cR, cC), inc=None,
                     reads=(P + "C",), writes=(P + "R",))
                tvv = s.tv(lambda e, cR=cR: e.tensor_scalar(
                    s.scrB[:, :], s.scrA[:, :], cR, 127.0,
                    op0=ALU.mult, op1=ALU.mult),
                    reads=("scrA", P + "R"), writes=("scrB",))
                s.scr_tok["B"] = tvv
                ta = s.ta(lambda e: e.activation(s.i32t[:, :], s.scrB[:, :],
                                                 AF.Identity),
                          waits=[tvv, s.i32_tok], reads=("scrB",),
                          writes=("i32",))
                s.i32_tok = s.tv(lambda e: e.tensor_copy(s.bstage[:, :],
                                                         s.i32t[:, :]),
                                 waits=[ta, s.bstage_tok],
                                 reads=("i32",), writes=("bst",))
                tr = s.tv(lambda e: e.tensor_tensor(s.scrA[:, :], s.scrB[:, :],
                                                    s.bstage[:, :],
                                                    op=ALU.subtract),
                          reads=("scrB", "bst"), writes=("scrA",))
                s.scr_tok["A"] = tr
                s.ta(lambda e: e.activation(s.sphi[:, :], s.scrA[:, :], AF.Sign,
                                            bias=s.cst[:, 0:1]),
                     waits=[tr, s.sphi_tok], inc=None,
                     reads=("scrA", "cst"), writes=("sph",))
                ta = s.ta(lambda e: e.activation(s.splo[:, :], s.scrA[:, :],
                                                 AF.Sign, bias=s.cst[:, 1:2]),
                          waits=[s.splo_tok],
                          reads=("scrA", "cst"), writes=("spl",))
                tvv = s.tv(lambda e: e.tensor_tensor(s.scrB[:, :], s.sphi[:, :],
                                                     s.splo[:, :], op=ALU.add),
                           waits=[ta], reads=("sph", "spl"), writes=("scrB",))
                s.scr_tok["B"] = tvv
                s.splo_tok = tvv
                tvv = s.tv(lambda e: e.tensor_scalar(s.scrA[:, :], s.scrB[:, :],
                                                     0.5, None, op0=ALU.mult),
                           reads=("scrB",), writes=("scrA",))
                s.scr_tok["A"] = tvv
                txq = s.tv(lambda e: e.tensor_tensor(s.bstage[:, :],
                                                     s.bstage[:, :], s.scrA[:, :],
                                                     op=ALU.add),
                           reads=("bst", "scrA"), writes=("bst",))
                items = [s.bstage[:, ic * 128:(ic + 1) * 128]
                         for ic in range(16)]

                def dst_fn(bi, pbap, ttok, tl_i=tl_i):
                    return s.tv(lambda e, pbap=pbap, bi=bi, tl_i=tl_i:
                                e.tensor_copy(
                                    s.xqTt[:, tl_i * 2048 + bi * 1024:
                                           tl_i * 2048 + (bi + 1) * 1024],
                                    pbap[:, 0:1024]),
                                waits=[ttok, xqT_read, s.arena_free],
                                reads=(), writes=("xqT:%d:%d" % (tl_i, bi),))

                dbg_st = None
                if s.debug and sq == 0:
                    dbg_st = s.store(s.dbg2_d[t], s.bstage[:, :], "st_dbg", waits=[txq])
                toks = s.bf16_transpose_batch(items, dst_fn,
                                              extra_waits=[txq, s.bstage_tok])
                s.bstage_tok = dbg_st if dbg_st is not None else toks[-1]
                s.sphi_tok = tvv
                s.splo_tok = tvv
                s.scr_tok["A"] = txq
                s.scr_tok["B"] = txq
                quant_tile[tl_i] = toks[-1]
                quant_last = toks[-1]
                for item in sched[tl_i]:
                    if item[0] == "mm":
                        emit_mm(item[1], item[2])
                    elif item[0] == "ev":
                        emit_ev(item[1], item[2])
                    else:
                        wf_load(item[1])
            if s.debug and sq == 0 and mt == 0:
                s.dbg_tok = s.store(s.dbg_d[:, 68:72], s.st4[:, 16:20],
                                    "st_dbg", waits=[quant_last])
            ttk = mm_tok[(3, 3)]
            xqT_read = ttk
            for tl_i in range(4):
                t = mt * 4 + tl_i
                last_out = s.store(
                    s.y_h[sq, t * 128:(t + 1) * 128, :],
                    s.x1m[:, tl_i * E:(tl_i + 1) * E],
                    "sty" if tl_i == 0 else "sty%d" % tl_i,
                    waits=[ev_fin[(3, tl_i)]])
                s.x1m_tok[tl_i] = last_out
        self.arena_free = ttk
        return last_out

    def build(self, phases="all"):
        self.setup()
        if phases == "all":
            for sq in range(NSEQ):
                p1_tok = self.p1(sq)
                p2_tok = self.p2(sq, p1_tok)
                self.p3(sq, p2_tok)
        for sn in ("st_wf", "st_jh", "st_jl", "stq_h", "stq_l", "stx",
                   "sty", "sty1", "sty2", "sty3", "st_dbg"):
            total = self.cnt[sn]
            if total:
                self.emit("a", lambda e, sn=sn, v=total: e.wait_ge(
                    self.sems[sn], v), inc=None, drain=False)
        with self.nc.Block() as block:
            for ename, attr in (("t", "tensor"), ("v", "vector"),
                                ("a", "scalar"), ("g", "gpsimd"), ("y", "sync")):
                ops = self.ops[ename]

                def body(eng, ops=ops):
                    for op in ops:
                        op(eng)

                getattr(block, attr)(body)
        return self.nc


_CACHE = {}


def _get_nc(debug=False, phases="all"):
    key = (debug, phases)
    if key not in _CACHE:
        _CACHE[key] = B(debug=debug).build(phases=phases)
    return _CACHE[key]


def kernel(L0_attn_x, L0_attn_Wq, L0_attn_bq, L0_attn_Wk, L0_attn_bk,
           L0_attn_Wv, L0_attn_bv, L0_ffn_weight):
    x = np.ascontiguousarray(np.asarray(L0_attn_x, dtype=np.float32))
    wq = np.ascontiguousarray(np.asarray(L0_attn_Wq, dtype=np.float32))
    bq = np.ascontiguousarray(np.asarray(L0_attn_bq, dtype=np.float32))
    wk = np.ascontiguousarray(np.asarray(L0_attn_Wk, dtype=np.float32))
    bk = np.ascontiguousarray(np.asarray(L0_attn_bk, dtype=np.float32))
    wv = np.ascontiguousarray(np.asarray(L0_attn_Wv, dtype=np.float32))
    bv = np.ascontiguousarray(np.asarray(L0_attn_bv, dtype=np.float32))
    wf = np.ascontiguousarray(np.asarray(L0_ffn_weight, dtype=np.float32))
    nc = _get_nc()
    in_maps = []
    for c in range(8):
        in_maps.append({"x": x[2 * c:2 * c + 2], "wq": wq, "bq": bq, "wk": wk,
                        "bk": bk, "wv": wv, "bv": bv, "wf": wf})
    res = run_bass_kernel_spmd(nc, in_maps, core_ids=list(range(8)))
    out = np.concatenate([r["y"] for r in res.results], axis=0)
    return out.reshape(16, S, E)


# version 19 (final)
# speedup vs baseline: 1.0151x; 1.0151x over previous
import numpy as np
import ml_dtypes
import concourse.bass as bass
import concourse.mybir as mybir
from concourse.bass_utils import run_bass_kernel_spmd

F32 = mybir.dt.float32
BF16 = mybir.dt.bfloat16
I32 = mybir.dt.int32
U8 = mybir.dt.uint8
AF = mybir.ActivationFunctionType
ALU = mybir.AluOpType

S, E, H, G, D = 2048, 2048, 16, 4, 128
NSEQ = 2
NT = 16
NMT = 4
NEC = E // 128
SCALE = 1.0 / float(np.sqrt(128.0))
SHIFT = -12.0
EPS = 1e-5
INV_E = 1.0 / float(E)


class B:
    def __init__(self, debug=False):
        self.debug = debug
        nc = self.nc = bass.Bass("TRN2", target_bir_lowering=False)
        self._guards = []
        self.sems = {}
        self.cnt = {}
        self.lastwait = {}
        self.ops = {e: [] for e in "tvagy"}
        self.eng_dirty = {e: set() for e in "vag"}
        self.eng_reads = {e: set() for e in "vag"}
        semnames = ["t", "v", "a", "g", "msc", "stq_h", "stq_l", "stx",
                    "sty", "sty1", "sty2", "sty3",
                    "st_wf", "st_jh", "st_jl", "st_dbg"]
        semnames += ["x1m%d" % i for i in range(4)]
        semnames += ["wqs0", "wqs1", "kvs0", "kvs1", "qs0", "qs1",
                     "wfs0", "wfs1"]
        for name in semnames:
            cm = nc.semaphore(name + "_sem")
            self.sems[name] = cm.__enter__()
            self._guards.append(cm)
            self.cnt[name] = 0
        self.declare_dram()
        self.alloc_sbuf()
        self.alloc_psum()
        self.arena_free = None
        self.x1m_tok = [None] * 4
        self.sphi_tok = None
        self.splo_tok = None
        self.scr_tok = {"A": None, "B": None}
        self.par_tok = {}
        self.bstage_tok = None
        self.i32_tok = None
        self.kv_read = None
        self.wq_slot = {}
        self.kv_slot = {}
        self.qr_slot = {}
        self.wf_slot = {}

    @staticmethod
    def _inter(a, b):
        if "*" in a:
            return bool(b)
        if "*" in b:
            return bool(a)
        return bool(a & b)

    def emit(self, eng, fn, waits=(), inc=None, inc_n=1, drain=True,
             reads=None, writes=None):
        if eng in ("v", "a", "g"):
            rd = set(reads) if reads is not None else {"*"}
            wr = set(writes) if writes is not None else {"*"}
            dirty = self.eng_dirty[eng]
            rdset = self.eng_reads[eng]
            need = (self._inter(rd, dirty) or self._inter(wr, dirty)
                    or self._inter(wr, rdset))
            if drain and need and self.ops[eng]:
                self.ops[eng].append(lambda engine: engine.drain())
                dirty.clear()
                rdset.clear()
            dirty |= wr
            rdset |= rd
        merged = {}
        for tok in waits:
            if tok is None:
                continue
            sn, val = tok
            if val <= 0:
                continue
            assert val <= self.cnt[sn], f"wait-on-future {sn}:{val}>{self.cnt[sn]}"
            merged[sn] = max(merged.get(sn, 0), val)
        wl = []
        for sn, val in merged.items():
            if val > self.lastwait.get((eng, sn), 0):
                wl.append((self.sems[sn], val))
                self.lastwait[(eng, sn)] = val
        tok = None
        sem = None
        if inc is not None:
            self.cnt[inc] += inc_n
            tok = (inc, self.cnt[inc])
            sem = self.sems[inc]

        def op(engine, wl=wl, fn=fn, sem=sem, inc_n=inc_n):
            for sh, v in wl:
                engine.wait_ge(sh, v)
            ins = fn(engine)
            if sem is not None:
                ins.then_inc(sem, inc_n)

        self.ops[eng].append(op)
        return tok

    def load(self, dst, src, sem, waits=()):
        return self.emit("y", lambda e, d=dst, s=src: e.dma_start(out=d, in_=s),
                         waits, inc=sem, inc_n=16)

    def gload(self, dst, src, sem, waits=()):
        return self.emit("g", lambda e, d=dst, s=src: e.dma_start(out=d, in_=s),
                         waits, inc=sem, inc_n=16, drain=False,
                         reads=(), writes=())

    def store(self, dst, src, sem, waits=()):
        return self.emit("a", lambda e, d=dst, s=src: e.dma_start(out=d, in_=s),
                         waits, inc=sem, inc_n=16, drain=False,
                         reads=(), writes=())

    def tv(self, fn, waits=(), inc="v", reads=None, writes=None):
        return self.emit("v", fn, waits, inc=inc, reads=reads, writes=writes)

    def ta(self, fn, waits=(), inc="a", reads=None, writes=None):
        return self.emit("a", fn, waits, inc=inc, reads=reads, writes=writes)

    def tt(self, fn, waits=(), inc="t"):
        return self.emit("t", fn, waits, inc=inc)

    def vgap(self):
        pass

    def declare_dram(self):
        nc = self.nc
        self.x_h = nc.dram_tensor("x", [NSEQ, S, E], F32, kind="ExternalInput").ap()
        self.wq_h = nc.dram_tensor("wq", [H, E, D], F32, kind="ExternalInput").ap()
        self.bq_h = nc.dram_tensor("bq", [H, D], F32, kind="ExternalInput").ap()
        self.wk_h = nc.dram_tensor("wk", [G, E, D], F32, kind="ExternalInput").ap()
        self.bk_h = nc.dram_tensor("bk", [G, D], F32, kind="ExternalInput").ap()
        self.wv_h = nc.dram_tensor("wv", [G, E, D], F32, kind="ExternalInput").ap()
        self.bv_h = nc.dram_tensor("bv", [G, D], F32, kind="ExternalInput").ap()
        self.wf_h = nc.dram_tensor("wf", [E, E], F32, kind="ExternalInput").ap()
        self.y_h = nc.dram_tensor("y", [NSEQ, S, E], F32, kind="ExternalOutput").ap()
        kd = "ExternalOutput" if self.debug else "Internal"
        self.wqh_d = nc.dram_tensor("wqh_d", [128, H * 2048], BF16, kind=kd).ap()
        self.wql_d = nc.dram_tensor("wql_d", [128, H * 2048], BF16,
                                    kind="Internal").ap()
        self.wkh_d = nc.dram_tensor("wkh_d", [128, G * 2048], BF16,
                                    kind="Internal").ap()
        self.wkl_d = nc.dram_tensor("wkl_d", [128, G * 2048], BF16,
                                    kind="Internal").ap()
        self.wvh_d = nc.dram_tensor("wvh_d", [128, G * 2048], BF16,
                                    kind="Internal").ap()
        self.wvl_d = nc.dram_tensor("wvl_d", [128, G * 2048], BF16,
                                    kind="Internal").ap()
        self.wfT_d = nc.dram_tensor("wfT_d", [128, E * 16], BF16, kind=kd).ap()
        self.qh_d = nc.dram_tensor("qh_d", [NSEQ, 128, H * 2048], BF16,
                                   kind=kd).ap()
        self.ql_d = nc.dram_tensor("ql_d", [NSEQ, 128, H * 2048], BF16,
                                   kind="Internal").ap()
        self.x1_d = nc.dram_tensor("x1_d", [NSEQ, NT, 128, E], F32, kind=kd).ap()
        self.dbg_d = nc.dram_tensor("dbg_d", [128, 80], F32, kind=kd).ap()
        self.dbg2_d = nc.dram_tensor("dbg2_d", [NT, 128, E], BF16, kind=kd).ap()
        self.dbg3_d = nc.dram_tensor("dbg3_d", [128, E], F32, kind=kd).ap()

    def alloc_sbuf(self):
        nc = self.nc
        A = nc.alloc_sbuf_tensor
        self.kch = A("kch", [128, G * S], BF16).ap()
        self.kcl = A("kcl", [128, G * S], BF16).ap()
        self.vch = A("vch", [128, G * S], BF16).ap()
        self.vcl = A("vcl", [128, G * S], BF16).ap()
        self.bqT = A("bqT", [128, H], F32).ap()
        self.bkT = A("bkT", [128, G], F32).ap()
        self.bvT = A("bvT", [128, G], F32).ap()
        self.identb = A("identb", [128, 128], BF16).ap()
        self.identf = A("identf", [128, 128], F32).ap()
        self.ohb = A("ohb", [128, H * 16], BF16).ap()
        self.mk = A("mk", [128, 4 * 512], BF16).ap()
        self.onesb = A("onesb", [128, 128], BF16).ap()
        self.wsc = A("wsc", [128, 4], F32).ap()
        self.cst = A("cst", [128, 4], F32).ap()
        self.st4 = A("st4", [128, 24], F32).ap()
        self.x1m = A("x1m", [128, 4 * E], F32).ap()
        self.scrA = A("scrA", [128, E], F32).ap()
        self.scrB = A("scrB", [128, E], F32).ap()
        self.sphi = A("sphi", [128, E], BF16).ap()
        self.splo = A("splo", [128, E], BF16).ap()
        self.bstage = A("bstage", [128, E], BF16).ap()
        self.i32t = A("i32t", [128, E], I32).ap()
        self.qstg = A("qstg", [128, 512], F32).ap()
        self.vstg = A("vstg", [128, 512], F32).ap()
        AR = 67584
        arena_h = A("arena", [128, AR], U8)
        base = nc.lookup_mloc(arena_h).addr

        def at(name, shape, dtype, off):
            h = nc.virtual_sbuf_tensor(name, shape, dtype)
            nc._tensor(h.name, h.shape, h.dtype, type="SB")
            mloc = nc.lookup_mloc(h)
            mloc.allocated = True
            mloc.addr = base + off
            per_part = int(np.prod(shape[1:])) * mybir.dt.size(dtype)
            assert off % 32 == 0 and off + per_part <= AR, name
            return h.ap()

        self.iosc = at("iosc", [128, 512], F32, 0)
        self.wqbh = at("wqbh", [128, 2 * 2048], BF16, 0)
        self.wqbl = at("wqbl", [128, 2 * 2048], BF16, 8192)
        self.wkvb = at("wkvb", [128, 4 * 2048], BF16, 16384)
        self.lnTh = at("lnTh", [128, NEC * 512], BF16, 32768)
        self.lnTl = at("lnTl", [128, NEC * 512], BF16, 49152)
        self.qhist = at("qhist", [128, 512], BF16, 65536)
        self.qlost = at("qlost", [128, 512], BF16, 66560)
        self.e32 = at("e32", [128, 4 * 512], F32, 0)
        self.ehi = at("ehi", [128, 4 * 512], BF16, 8192)
        self.elo = at("elo", [128, 4 * 512], BF16, 12288)
        self.qrh = at("qrh", [128, 2 * 512], BF16, 16384)
        self.qrl = at("qrl", [128, 2 * 512], BF16, 18432)
        self.outT = at("outT", [128, H * 512], F32, 20480)
        self.Rsb = at("Rsb", [128, 512], F32, 53248)
        self.RTsb = at("RTsb", [128, 64], F32, 55296)
        self.wfb = at("wfb", [128, 2 * 8192], BF16, 0)
        self.xqTt = at("xqTt", [128, 4 * 2048], BF16, 32768)

    def alloc_psum(self):
        P = self.nc.alloc_psum_tensor
        self.scores4 = P("scores4", [128, 2048], F32).ap()
        self.pO1 = P("pO1", [128, 512], F32).ap()
        self.pS = P("pS", [128, 512], F32).ap()
        self.pb = {"pb1": P("pb1", [128, 1024], BF16).ap(),
                   "pb2": P("pb2", [128, 1024], BF16).ap()}
        self.bank_w = {}
        self.bank_r = {}
        self._pbn = ["pb1", "pb2"]

    def sbank(self, i):
        return self.scores4[:, i * 512:(i + 1) * 512]

    def bf16_transpose_batch(self, items, dst_fn, extra_waits=()):
        toks = []
        for b0 in range(0, len(items), 8):
            chunk = items[b0:b0 + 8]
            pbn = self._pbn[0]
            self._pbn.reverse()
            pbap = self.pb[pbn]
            t = None
            n = len(chunk)
            for i, srcap in enumerate(chunk):
                w = (list(extra_waits) + [self.bank_r.get(pbn)]) if i == 0 else []
                t = self.tt(lambda e, pbap=pbap, i=i, srcap=srcap, n=n:
                    e.matmul(pbap[:, i * 128:(i + 1) * 128], srcap,
                             self.identb[:, :], start=(i == 0),
                             stop=(i == n - 1), is_transpose=True),
                    waits=w, inc="t" if i == n - 1 else None)
            self.bank_w[pbn] = t
            tv = dst_fn(b0 // 8, pbap, t)
            self.bank_r[pbn] = tv
            toks.append(tv)
        return toks

    def setup(self):
        s = self
        s.tv(lambda e: e.memset(s.cst[:, 0:1], 0.5), inc=None)
        s.tv(lambda e: e.memset(s.cst[:, 1:2], -0.5), inc=None)
        s.tv(lambda e: e.memset(s.cst[:, 2:3], EPS), inc=None)
        self.cst_tok = s.tv(lambda e: e.memset(s.cst[:, 3:4], SHIFT))
        tg = s.emit("g", lambda e: e.iota(s.iosc[:, 0:128], pattern=[[0, 128]],
                                          base=0, channel_multiplier=1,
                                          allow_small_or_imprecise_dtypes=True),
                    inc="g")
        tg = s.emit("g", lambda e: e.iota(s.iosc[:, 128:256], pattern=[[1, 128]],
                                          base=0, channel_multiplier=0,
                                          allow_small_or_imprecise_dtypes=True),
                    inc="g")
        iop = s.iosc[:, 0:128]
        ioj = s.iosc[:, 128:256]
        s.tv(lambda e: e.tensor_tensor(s.identf[:, :], ioj, iop, op=ALU.is_equal),
             waits=[tg], inc=None)
        s.tv(lambda e: e.tensor_copy(s.identb[:, :], s.identf[:, :]), inc=None)
        s.tv(lambda e: e.memset(s.onesb[:, :], 1.0), inc=None)
        for h in range(H):
            s.tv(lambda e, h=h: e.tensor_scalar(s.ohb[:, h * 16:(h + 1) * 16],
                                                ioj[:, 0:16], float(h), None,
                                                op0=ALU.is_equal), inc=None)
        tv_id = s.tv(lambda e: e.tensor_copy(s.iosc[:, 256:257], s.identf[:, 0:1]))
        tv_prev = tv_id
        for i in range(4):
            tgi = s.emit("g", lambda e, i=i: e.iota(
                s.iosc[:, 0:512], pattern=[[1, 512]], base=-128 * i,
                channel_multiplier=-1, allow_small_or_imprecise_dtypes=True),
                waits=[tv_prev], inc="g")
            tv_prev = s.tv(lambda e, i=i: e.tensor_scalar(
                s.mk[:, i * 512:(i + 1) * 512], s.iosc[:, 0:512], 0.0, None,
                op0=ALU.is_ge), waits=[tgi])
        tl1 = s.load(s.iosc[0:16, 0:128], s.bq_h[:, :], "msc", waits=[tv_prev])
        tl2 = s.load(s.iosc[0:4, 128:256], s.bk_h[:, :], "msc")
        tl3 = s.load(s.iosc[0:4, 256:384], s.bv_h[:, :], "msc")
        s.tt(lambda e: e.matmul(s.pO1[:, 0:16], s.iosc[0:16, 0:128],
                                s.identf[0:16, 0:16], start=True, stop=False,
                                is_transpose=True),
             waits=[tl1, tl2, tl3, tv_id], inc=None)
        s.tt(lambda e: e.matmul(s.pO1[:, 16:20], s.iosc[0:4, 128:256],
                                s.identf[0:4, 0:4], start=False, stop=False,
                                is_transpose=True), inc=None)
        ttk = s.tt(lambda e: e.matmul(s.pO1[:, 32:36], s.iosc[0:4, 256:384],
                                      s.identf[0:4, 0:4], start=False,
                                      stop=True, is_transpose=True))
        s.tv(lambda e: e.tensor_copy(s.bqT[:, :], s.pO1[:, 0:16]), waits=[ttk],
             inc=None)
        s.tv(lambda e: e.tensor_copy(s.bkT[:, :], s.pO1[:, 16:20]), inc=None)
        tvb = s.tv(lambda e: e.tensor_copy(s.bvT[:, :], s.pO1[:, 32:36]))
        s.bank_r["pO1"] = tvb
        s.ta(lambda e: e.preload_act_table(AF.Sqrt), inc=None)

        tz = s.tv(lambda e: e.memset(s.st4[:, 0:16], 0.0))
        wslot = [None, None]
        ta = None
        for oc in range(16):
            sl = oc % 2
            xap = s.x1m[:, sl * E:(sl + 1) * E]
            tl = s.load(xap, s.wf_h[oc * 128:(oc + 1) * 128, :], "x1m%d" % sl,
                        waits=[wslot[sl]])
            ta = s.ta(lambda e, xap=xap, oc=oc: e.activation(
                s.scrA[:, :], xap, AF.Abs, accum_out=s.st4[:, oc:oc + 1]),
                waits=[tl, tz])
            wslot[sl] = ta
        s.tv(lambda e: e.tensor_copy(s.sphi[:, 0:16], s.st4[:, 0:16]),
             waits=[ta], inc=None)
        s.vgap()
        s.tv(lambda e: e.tensor_copy(s.iosc[:, 384:400], s.sphi[:, 0:16]),
             inc=None)
        s.vgap()
        s.tv(lambda e: e.tensor_tensor(s.iosc[:, 400:416], s.st4[:, 0:16],
                                       s.iosc[:, 384:400], op=ALU.subtract),
             inc=None)
        s.vgap()
        tlo = s.tv(lambda e: e.tensor_copy(s.splo[:, 0:16], s.iosc[:, 400:416]))
        s.tt(lambda e: e.matmul(s.pO1[:, 0:16], s.onesb[:, :], s.sphi[:, 0:16],
                                start=True, stop=False),
             waits=[tlo, s.bank_r["pO1"]], inc=None)
        ttk = s.tt(lambda e: e.matmul(s.pO1[:, 0:16], s.onesb[:, :],
                                      s.splo[:, 0:16], start=False, stop=True))
        s.bank_w["pO1"] = ttk
        tv1 = s.tv(lambda e: e.pool_avg(s.st4[:, 0:1], s.pO1[:, 0:16]),
                   waits=[ttk])
        s.bank_r["pO1"] = tv1
        ta1 = s.ta(lambda e: e.activation(s.st4[:, 1:2], s.st4[:, 0:1],
                                          AF.Identity), waits=[tv1])
        s.tv(lambda e: e.tensor_scalar(s.wsc[:, 0:1], s.st4[:, 1:2],
                                       float(16.0 / (E * E)), 1e-6,
                                       op0=ALU.mult, op1=ALU.max),
             waits=[ta1], inc=None)
        s.vgap()
        s.tv(lambda e: e.reciprocal(s.wsc[:, 1:2], s.wsc[:, 0:1]), inc=None)
        tv_ws = s.tv(lambda e: e.tensor_scalar(s.wsc[:, 2:3], s.wsc[:, 0:1],
                                               float(1.0 / 127.0), None,
                                               op0=ALU.mult))
        if s.debug:
            s.tv(lambda e: e.tensor_copy(s.iosc[:, 0:16], s.st4[:, 0:16]),
                 inc=None)
            s.tv(lambda e: e.tensor_copy(s.iosc[:, 16:32], s.pO1[:, 0:16]),
                 inc=None)
            s.tv(lambda e: e.tensor_copy(s.iosc[:, 32:36], s.wsc[:, 0:4]),
                 inc=None)
            s.tv(lambda e: e.tensor_copy(s.iosc[:, 36:52], s.sphi[:, 0:16]),
                 inc=None)
            tdg = s.tv(lambda e: e.tensor_copy(s.iosc[:, 52:68], s.splo[:, 0:16]))
            s.store(s.dbg_d[:, 0:68], s.iosc[:, 0:68], "st_dbg", waits=[tdg])

        wslot = [None, None]
        st_done = None
        for oc in range(16):
            sl = oc % 2
            xap = s.x1m[:, sl * E:(sl + 1) * E]
            tl = s.load(xap, s.wf_h[oc * 128:(oc + 1) * 128, :], "x1m%d" % sl,
                        waits=[wslot[sl], tv_ws])
            tvv = s.tv(lambda e, xap=xap: e.tensor_scalar(
                s.scrB[:, :], xap, s.wsc[:, 1:2], None, op0=ALU.mult),
                waits=[tl, s.scr_tok["B"]])
            wslot[sl] = tvv
            ta = s.ta(lambda e: e.activation(s.i32t[:, :], s.scrB[:, :],
                                             AF.Identity),
                      waits=[tvv, s.i32_tok])
            s.i32_tok = s.tv(lambda e: e.tensor_copy(s.bstage[:, :],
                                                     s.i32t[:, :]),
                             waits=[ta, s.bstage_tok])
            tvv = s.tv(lambda e: e.tensor_tensor(s.scrA[:, :], s.scrB[:, :],
                                                 s.bstage[:, :],
                                                 op=ALU.subtract))
            s.ta(lambda e: e.activation(s.sphi[:, :], s.scrA[:, :], AF.Sign,
                                        bias=s.cst[:, 0:1]),
                 waits=[tvv, s.sphi_tok], inc=None)
            ta = s.ta(lambda e: e.activation(s.splo[:, :], s.scrA[:, :], AF.Sign,
                                             bias=s.cst[:, 1:2]), waits=[s.splo_tok])
            tvv = s.tv(lambda e: e.tensor_tensor(s.scrB[:, :], s.sphi[:, :],
                                                 s.splo[:, :], op=ALU.add),
                       waits=[ta])
            s.scr_tok["B"] = tvv
            tvv = s.tv(lambda e: e.tensor_scalar(s.scrA[:, :], s.scrB[:, :], 0.5,
                                                 None, op0=ALU.mult))
            s.scr_tok["A"] = tvv
            tvv = s.tv(lambda e: e.tensor_tensor(s.scrB[:, :], s.bstage[:, :],
                                                 s.scrA[:, :], op=ALU.add))
            tvq = s.tv(lambda e: e.tensor_scalar(s.bstage[:, :], s.scrB[:, :],
                                                 -1.0, 1.0, op0=ALU.max,
                                                 op1=ALU.min))
            items = [s.bstage[:, ic * 128:(ic + 1) * 128] for ic in range(16)]

            def dst_fn(bi, pbap, ttok, prev=st_done):
                return s.tv(lambda e, bi=bi, pbap=pbap: e.tensor_copy(
                    s.sphi[:, bi * 1024:(bi + 1) * 1024], pbap[:, 0:1024]),
                    waits=[ttok, prev])

            toks = s.bf16_transpose_batch(items, dst_fn, extra_waits=[tvq])
            s.bstage_tok = toks[-1]
            s.scr_tok["A"] = tvq
            s.scr_tok["B"] = tvq
            og, osub = oc // 4, oc % 4
            dst = s.wfT_d.rearrange("p (og ic osub ol) -> p og ic osub ol",
                                    og=4, ic=16, osub=4)[:, og, :, osub, :]
            src = s.sphi.rearrange("p (ic ol) -> p ic ol", ic=16)
            st_done = s.store(dst, src, "st_wf", waits=toks)
            s.sphi_tok = st_done

        jobs = [(s.wq_h, h, s.wqh_d, s.wql_d, h) for h in range(H)]
        jobs += [(s.wk_h, g, s.wkh_d, s.wkl_d, g) for g in range(G)]
        jobs += [(s.wv_h, g, s.wvh_d, s.wvl_d, g) for g in range(G)]
        sp_st = [st_done, st_done]
        for j, (src_h, idx, dh, dl, col) in enumerate(jobs):
            sl = j % 2
            xap = s.x1m[:, sl * E:(sl + 1) * E]
            src = src_h.rearrange("h (ec p) d -> h p ec d", p=128)[idx]
            tl = s.load(xap.rearrange("p (ec d) -> p ec d", ec=16), src,
                        "x1m%d" % sl, waits=[wslot[sl]])
            thi = s.tv(lambda e, xap=xap: e.tensor_copy(s.sphi[:, :], xap),
                       waits=[tl, sp_st[0], s.sphi_tok])
            tlo = s.tv(lambda e, xap=xap: e.tensor_tensor(
                s.splo[:, :], xap, s.sphi[:, :], op=ALU.subtract),
                waits=[sp_st[1], s.splo_tok])
            s.sphi_tok = None
            s.splo_tok = None
            wslot[sl] = tlo
            sp_st[0] = s.store(dh[:, col * 2048:(col + 1) * 2048], s.sphi[:, :],
                               "st_jh", waits=[thi])
            sp_st[1] = s.store(dl[:, col * 2048:(col + 1) * 2048], s.splo[:, :],
                               "st_jl", waits=[tlo])
        self.setup_done = None
        self.setup_done_multi = [("st_wf", self.cnt["st_wf"]),
                                 ("st_jh", self.cnt["st_jh"]),
                                 ("st_jl", self.cnt["st_jl"]), sp_st[0]]
        self.sphi_tok = sp_st[0]
        self.splo_tok = sp_st[1]
        self.x1m_tok = [wslot[0], wslot[1], None, None]

    def p1(self, sq):
        s = self
        s.ta(lambda e: e.preload_act_table(AF.Sqrt), inc=None)
        bank_names = ["s0", "s1", "s2", "s3"]
        lnT_read = None
        cache_tok = None
        ttk = None
        for mt in range(NMT):
            lnT_evacs = []
            for tl_i in range(4):
                t = mt * 4 + tl_i
                xap = s.x1m[:, tl_i * E:(tl_i + 1) * E]
                tld = s.load(xap, s.x_h[sq, t * 128:(t + 1) * 128, :],
                             "x1m%d" % tl_i, waits=[s.x1m_tok[tl_i]])
                par = (t % 2) * 8
                P = "st:%d" % par
                mean_ap, rstd_ap, tst = self.ln_stats_waitload(xap, par, tld)
                tln = s.tv(lambda e, xap=xap, m=mean_ap, r=rstd_ap:
                           e.tensor_scalar(s.scrA[:, :], xap, m, r,
                                           op0=ALU.subtract, op1=ALU.mult),
                           waits=[tst, s.scr_tok["A"]],
                           reads=("x1m:%d" % tl_i, P + "m", P + "v"),
                           writes=("scrA",))
                s.x1m_tok[tl_i] = tln
                thi = s.tv(lambda e: e.tensor_copy(s.sphi[:, :], s.scrA[:, :]),
                           waits=[s.sphi_tok], reads=("scrA",), writes=("sph",))
                tlo = s.tv(lambda e: e.tensor_tensor(
                    s.splo[:, :], s.scrA[:, :], s.sphi[:, :], op=ALU.subtract),
                    waits=[s.splo_tok], reads=("scrA", "sph"), writes=("spl",))
                s.scr_tok["A"] = tlo
                toks = None
                for (srcb, dstb, half) in ((s.sphi, s.lnTh, 0), (s.sphi, s.lnTh, 1),
                                           (s.splo, s.lnTl, 0), (s.splo, s.lnTl, 1)):
                    items = [srcb[:, (half * 8 + i) * 128:(half * 8 + i + 1) * 128]
                             for i in range(8)]
                    dst = dstb.rearrange("p (ec q) -> p ec q", ec=16)[
                        :, half * 8:half * 8 + 8, tl_i * 128:(tl_i + 1) * 128]

                    def dst_fn(bi, pbap, ttok, dst=dst, srcb=srcb, half=half):
                        key = "lnT:%d:%d" % (id(srcb) % 97, half)
                        return s.tv(lambda e, pbap=pbap, dst=dst: e.tensor_copy(
                            dst, pbap.rearrange("p (i q) -> p i q", i=8)),
                            waits=[ttok, lnT_read, s.arena_free],
                            reads=(), writes=(key,))

                    toks = s.bf16_transpose_batch(items, dst_fn,
                                                  extra_waits=[tlo])
                    lnT_evacs += toks
                s.sphi_tok = toks[-1]
                s.splo_tok = toks[-1]
            lnT_ready = lnT_evacs[-1]

            units = [("q", h) for h in range(H)] + \
                    [("k", g) for g in range(G)] + [("v", g) for g in range(G)]
            ld_toks = {}

            def emit_unit_load(kind, idx):
                if kind == "q":
                    sl = idx % 2
                    ld_toks[(kind, idx)] = (
                        s.gload(s.wqbh[:, sl * 2048:(sl + 1) * 2048],
                               s.wqh_d[:, idx * 2048:(idx + 1) * 2048],
                               "wqs%d" % sl,
                               waits=[s.wq_slot.get(sl), s.arena_free]
                                     + s.setup_done_multi),
                        s.gload(s.wqbl[:, sl * 2048:(sl + 1) * 2048],
                               s.wql_d[:, idx * 2048:(idx + 1) * 2048],
                               "wqs%d" % sl))
                else:
                    sl = idx % 2
                    hi_d = s.wkh_d if kind == "k" else s.wvh_d
                    lo_d = s.wkl_d if kind == "k" else s.wvl_d
                    ld_toks[(kind, idx)] = (
                        s.gload(s.wkvb[:, (sl * 2) * 2048:(sl * 2 + 1) * 2048],
                               hi_d[:, idx * 2048:(idx + 1) * 2048],
                               "kvs%d" % sl,
                               waits=[s.kv_slot.get(sl), s.arena_free]
                                     + s.setup_done_multi),
                        s.gload(s.wkvb[:, (sl * 2 + 1) * 2048:(sl * 2 + 2) * 2048],
                               lo_d[:, idx * 2048:(idx + 1) * 2048],
                               "kvs%d" % sl))

            emit_unit_load(*units[0])
            emit_unit_load(*units[1])
            qst_free = None
            qhist_free = ("stq_h", s.cnt["stq_h"])
            qlost_free = ("stq_l", s.cnt["stq_l"])
            vstg_free = None
            for ui, (kind, idx) in enumerate(units):
                bn = bank_names[ui % 4]
                bap = s.sbank(ui % 4)
                lt = ld_toks[(kind, idx)]
                if kind == "q":
                    whi = s.wqbh[:, (idx % 2) * 2048:((idx % 2) + 1) * 2048]
                    wlo = s.wqbl[:, (idx % 2) * 2048:((idx % 2) + 1) * 2048]
                else:
                    sl = (idx % 2) * 2
                    whi = s.wkvb[:, sl * 2048:(sl + 1) * 2048]
                    wlo = s.wkvb[:, (sl + 1) * 2048:(sl + 2) * 2048]
                first = True
                for ec in range(NEC):
                    for pi, (wap, rap) in enumerate(
                            ((whi, s.lnTh), (whi, s.lnTl), (wlo, s.lnTh))):
                        w = [lt[0], lt[1], lnT_ready,
                             s.bank_r.get(bn)] if first else []
                        last = (ec == NEC - 1) and (pi == 2)
                        ttk = s.tt(lambda e, bap=bap, wap=wap, rap=rap, ec=ec,
                                   first=first, last=last: e.matmul(
                                       bap, wap[:, ec * 128:(ec + 1) * 128],
                                       rap[:, ec * 512:(ec + 1) * 512],
                                       start=first, stop=last),
                                   waits=w, inc="t" if last else None)
                        first = False
                s.bank_w[bn] = ttk
                if kind == "q":
                    s.wq_slot[idx % 2] = ttk
                else:
                    s.kv_slot[idx % 2] = ttk
                if ui + 2 < len(units):
                    emit_unit_load(*units[ui + 2])
                if kind == "q":
                    tq = s.tv(lambda e, bap=bap, idx=idx: e.tensor_scalar(
                        s.qstg[:, :], bap, s.bqT[:, idx:idx + 1], None,
                        op0=ALU.add), waits=[ttk, qst_free],
                        reads=("bq",), writes=("qstg",))
                    s.bank_r[bn] = tq
                    thi = s.tv(lambda e: e.tensor_copy(s.qhist[:, :],
                                                       s.qstg[:, :]),
                               waits=[qhist_free, s.arena_free],
                               reads=("qstg",), writes=("qh",))
                    tlo = s.tv(lambda e: e.tensor_tensor(
                        s.qlost[:, :], s.qstg[:, :], s.qhist[:, :],
                        op=ALU.subtract), waits=[qlost_free],
                        reads=("qstg", "qh"), writes=("ql",))
                    qst_free = tlo
                    base = idx * 2048 + mt * 512
                    qhist_free = s.store(s.qh_d[sq, :, base:base + 512],
                                         s.qhist[:, :], "stq_h", waits=[thi])
                    qlost_free = s.store(s.ql_d[sq, :, base:base + 512],
                                         s.qlost[:, :], "stq_l", waits=[tlo])
                elif kind == "k":
                    tq = s.tv(lambda e, bap=bap, idx=idx: e.tensor_scalar(
                        s.qstg[:, :], bap, s.bkT[:, idx:idx + 1], None,
                        op0=ALU.add), waits=[ttk, qst_free, s.kv_read],
                        reads=("bk",), writes=("qstg",))
                    s.bank_r[bn] = tq
                    dh = s.kch[:, idx * 2048 + mt * 512:idx * 2048 + (mt + 1) * 512]
                    dl = s.kcl[:, idx * 2048 + mt * 512:idx * 2048 + (mt + 1) * 512]
                    s.tv(lambda e, dh=dh: e.tensor_copy(dh, s.qstg[:, :]),
                         inc=None, reads=("qstg",), writes=("kch",))
                    cache_tok = s.tv(lambda e, dh=dh, dl=dl: e.tensor_tensor(
                        dl, s.qstg[:, :], dh, op=ALU.subtract),
                        reads=("qstg", "kch"), writes=("kcl",))
                    qst_free = cache_tok
                else:
                    tq = s.tv(lambda e, bap=bap, idx=idx: e.tensor_scalar(
                        s.vstg[:, :], bap, s.bvT[:, idx:idx + 1], None,
                        op0=ALU.add),
                        waits=[ttk, vstg_free, s.kv_read,
                               getattr(s, "vstg_g", None)],
                        reads=("bv",), writes=("vstg",))
                    s.bank_r[bn] = tq
                    ttv = None
                    for sub in range(4):
                        w = [tq, s.bank_r.get("pO1")] if sub == 0 else []
                        ttv = s.tt(lambda e, sub=sub: e.matmul(
                            s.pO1[:, sub * 128:(sub + 1) * 128],
                            s.vstg[:, sub * 128:(sub + 1) * 128],
                            s.identf[:, :], start=(sub == 0), stop=(sub == 3),
                            is_transpose=True),
                            waits=w, inc="t" if sub == 3 else None)
                    s.bank_w["pO1"] = ttv
                    vstg_free = ttv
                    base = (idx * 16 + mt * 4) * 128
                    s.tv(lambda e, base=base: e.tensor_copy(
                        s.vch[:, base:base + 512], s.pO1[:, 0:512]),
                        waits=[ttv], inc=None, reads=(), writes=("vch",))
                    cache_tok = s.tv(lambda e, base=base: e.tensor_tensor(
                        s.vcl[:, base:base + 512], s.pO1[:, 0:512],
                        s.vch[:, base:base + 512], op=ALU.subtract),
                        reads=("vch",), writes=("vcl",))
                    s.bank_r["pO1"] = cache_tok
            lnT_read = ttk
        self.arena_free = ttk
        return cache_tok

    def ln_stats_waitload(self, xap, par, tld):
        s = self
        P = "st:%d" % par
        cc = [s.st4[:, par + i:par + i + 1] for i in range(6)]
        xk = "x1m:*"
        tz = s.tv(lambda e, par=par: e.memset(s.st4[:, par:par + 2], 0.0),
                  waits=[s.par_tok.get(par)], reads=(), writes=(P + "a",))
        s.ta(lambda e, xap=xap, c=cc[0]: e.activation(
            s.scrB[:, :], xap, AF.Identity, accum_out=c),
            waits=[tld, tz, s.scr_tok["B"]], inc=None,
            reads=(xk,), writes=("scrB", P + "a"))
        ta = s.ta(lambda e, xap=xap, c=cc[1]: e.activation(
            s.scrB[:, :], xap, AF.Square, accum_out=c),
            reads=(xk,), writes=("scrB", P + "a"))
        s.par_tok[par] = ta
        s.scr_tok["B"] = ta
        s.tv(lambda e, c=cc: e.tensor_scalar(c[2], c[0], INV_E, None,
                                             op0=ALU.mult), waits=[ta],
             inc=None, reads=(P + "a",), writes=(P + "m",))
        s.tv(lambda e, c=cc: e.tensor_scalar(c[3], c[1], INV_E, None,
                                             op0=ALU.mult), inc=None,
             reads=(P + "a",), writes=(P + "q",))
        s.tv(lambda e, c=cc: e.tensor_tensor(c[4], c[2], c[2], op=ALU.mult),
             inc=None, reads=(P + "m",), writes=(P + "v",))
        tvv = s.tv(lambda e, c=cc: e.tensor_tensor(c[4], c[3], c[4],
                                                   op=ALU.subtract),
                   reads=(P + "q", P + "v"), writes=(P + "v",))
        ta = s.ta(lambda e, c=cc: e.activation(c[5], c[4], AF.Sqrt,
                                               bias=s.cst[:, 2:3]),
                  waits=[tvv], reads=(P + "v", "cst"), writes=(P + "s",))
        tst = s.tv(lambda e, c=cc: e.reciprocal(c[4], c[5]), waits=[ta],
                   reads=(P + "s",), writes=(P + "v",))
        return cc[2], cc[4], tst

    def p2(self, sq, p1_tok):
        s = self
        s.ta(lambda e: e.preload_act_table(AF.Exp), inc=None)
        e32_free = {}
        e_used = {}
        exp_done = {}
        last_store = None
        for mt in range(NMT):
            C = 4 * (mt + 1)
            x1_ld = []
            for tl_i in range(4):
                x1_ld.append(s.load(
                    s.x1m[:, tl_i * E:(tl_i + 1) * E],
                    s.x_h[sq, (mt * 4 + tl_i) * 128:(mt * 4 + tl_i + 1) * 128, :],
                    "x1m%d" % tl_i, waits=[s.x1m_tok[tl_i]]))
            q_ld = {}

            def q_load(h, mt=mt):
                sl = h % 2
                base = h * 2048 + mt * 512
                q_ld[h] = (
                    s.load(s.qrh[:, sl * 512:(sl + 1) * 512],
                           s.qh_d[sq, :, base:base + 512], "qs%d" % sl,
                           waits=[s.qr_slot.get(sl), p1_tok, s.arena_free,
                                  ("stq_h", s.cnt["stq_h"]),
                                  ("stq_l", s.cnt["stq_l"])]),
                    s.load(s.qrl[:, sl * 512:(sl + 1) * 512],
                           s.ql_d[sq, :, base:base + 512], "qs%d" % sl))

            q_load(0)
            q_load(1)
            sums_started = [False]
            last_sums = None
            out_evacs = []
            for h in range(H):
                g = h // 4
                qh = s.qrh[:, (h % 2) * 512:((h % 2) + 1) * 512]
                ql = s.qrl[:, (h % 2) * 512:((h % 2) + 1) * 512]
                obn = "pO1" if h % 2 == 0 else "s3"
                obap = s.pO1 if h % 2 == 0 else s.sbank(3)
                split_args = {}

                def emit_scores(c, h=h, g=g, qh=qh, ql=ql, mt=mt):
                    bn = c % 3
                    band_i = c - 4 * mt
                    q0 = 128 * band_i if band_i > 0 else 0
                    bap = s.sbank(bn)[:, q0:512]
                    kh = s.kch[:, g * 2048 + c * 128:g * 2048 + (c + 1) * 128]
                    kl = s.kcl[:, g * 2048 + c * 128:g * 2048 + (c + 1) * 128]
                    w0 = [q_ld[h][0], q_ld[h][1], exp_done.get(bn),
                          s.bank_r.get("s%d" % bn), p1_tok]
                    tk = None
                    for pi, (wap, rap) in enumerate(
                            ((kh, qh[:, q0:512]), (kh, ql[:, q0:512]),
                             (kl, qh[:, q0:512]))):
                        tk = s.tt(lambda e, bap=bap, wap=wap, rap=rap, pi=pi:
                                  e.matmul(bap, wap, rap, start=(pi == 0),
                                           stop=(pi == 2)),
                                  waits=w0 if pi == 0 else [],
                                  inc="t" if pi == 2 else None)
                    s.bank_w["s%d" % bn] = tk
                    esl = c % 4
                    ta = s.ta(lambda e, bap=bap, esl=esl, q0=q0: e.activation(
                        s.e32[:, esl * 512 + q0:(esl + 1) * 512], bap, AF.Exp,
                        bias=s.cst[:, 3:4], scale=SCALE),
                        waits=[tk, e32_free.get(esl), s.arena_free],
                        reads=("cst",), writes=("e32:%d" % esl,))
                    exp_done[bn] = ta
                    e32ap = s.e32[:, esl * 512 + q0:(esl + 1) * 512]
                    ehiap = s.ehi[:, esl * 512 + q0:(esl + 1) * 512]
                    eloap = s.elo[:, esl * 512 + q0:(esl + 1) * 512]
                    wsp = [ta, e_used.get(esl), s.arena_free]
                    split_args[c] = (esl, band_i, q0, e32ap, ehiap, eloap, wsp)

                def emit_pv(c, h=h, g=g, obap=obap, obn=obn, C=C):
                    esl, band_i, q0, e32ap, ehiap, eloap, wsp = \
                        split_args.pop(c)
                    ek = "e32:%d" % esl
                    hk = "eh:%d" % esl
                    lk = "el:%d" % esl
                    W = 512 - q0
                    if band_i >= 0:
                        mki = s.mk[:, band_i * 512 + q0:(band_i + 1) * 512]
                        s.tv(lambda e, a=ehiap, b=e32ap, m=mki:
                             e.tensor_tensor(a, b, m, op=ALU.mult),
                             waits=wsp, inc=None, reads=(ek, "mk"),
                             writes=(hk,))
                        s.tv(lambda e, b=e32ap, hh=ehiap, W=W:
                             e.tensor_tensor(s.qstg[:, 0:W], b, hh,
                                             op=ALU.subtract),
                             inc=None, reads=(ek, hk), writes=("qstg",))
                        tvv = s.tv(lambda e, a=eloap, m=mki, W=W:
                                   e.tensor_tensor(a, s.qstg[:, 0:W], m,
                                                   op=ALU.mult),
                                   reads=("qstg", "mk"), writes=(lk,))
                    else:
                        s.tv(lambda e, a=ehiap, b=e32ap: e.tensor_copy(a, b),
                             waits=wsp, inc=None, reads=(ek,), writes=(hk,))
                        tvv = s.tv(lambda e, a=eloap, b=e32ap, hh=ehiap:
                                   e.tensor_tensor(a, b, hh,
                                                   op=ALU.subtract),
                                   reads=(ek, hk), writes=(lk,))
                    e32_free[esl] = tvv
                    vh = s.vch[:, (g * 16 + c) * 128:(g * 16 + c + 1) * 128]
                    vl = s.vcl[:, (g * 16 + c) * 128:(g * 16 + c + 1) * 128]
                    ob = obap[:, q0:512]
                    w0 = [e32_free[esl]]
                    if c == 0:
                        w0.append(s.bank_r.get(obn))
                    for pi, (wap, rap) in enumerate(((vh, ehiap), (vl, ehiap),
                                                     (vh, eloap))):
                        s.tt(lambda e, ob=ob, wap=wap, rap=rap, pi=pi, c=c:
                             e.matmul(ob, wap, rap,
                                      start=(c == 0 and pi == 0),
                                      stop=(c == C - 1 and pi == 2)),
                             waits=w0 if pi == 0 else [], inc=None)
                    oh = s.ohb[:, h * 16:(h + 1) * 16]
                    st0 = not sums_started[0]
                    wS = [s.bank_r.get("pS")] if st0 else []
                    s.tt(lambda e, oh=oh, rap=ehiap, st0=st0, q0=q0: e.matmul(
                        s.pS[0:16, q0:512], oh, rap, start=st0, stop=False),
                        waits=wS, inc=None)
                    laststop = (h == H - 1 and c == C - 1)
                    tk = s.tt(lambda e, oh=oh, rap=eloap, ls=laststop, q0=q0:
                              e.matmul(s.pS[0:16, q0:512], oh, rap,
                                       start=False, stop=ls))
                    sums_started[0] = True
                    e_used[esl] = tk
                    return tk

                last_pv = None
                LAG = 2
                for c in range(C):
                    emit_scores(c)
                    if c >= LAG:
                        last_pv = emit_pv(c - LAG)
                for c in range(max(0, C - LAG), C):
                    last_pv = emit_pv(c)
                last_sums = last_pv
                s.bank_w[obn] = last_pv
                tv = s.ta(lambda e, obap=obap, h=h: e.activation(
                    s.outT[:, h * 512:(h + 1) * 512], obap[:, 0:512],
                    AF.Identity),
                    waits=[last_pv, s.arena_free],
                    reads=(), writes=("outT:%d" % h,))
                s.bank_r[obn] = tv
                s.qr_slot[h % 2] = tv
                out_evacs.append(tv)
                if h + 2 < H:
                    q_load(h + 2)
            s.bank_w["pS"] = last_sums

            trc = s.tv(lambda e: e.reciprocal(s.Rsb[0:16, :], s.pS[0:16, :]),
                       waits=[last_sums, s.arena_free],
                       reads=(), writes=("Rsb",))
            s.bank_r["pS"] = trc
            ttk = None
            for sub in range(4):
                w = [trc, exp_done.get(2), s.bank_r.get("s2")] if sub == 0 else []
                ttk = s.tt(lambda e, sub=sub: e.matmul(
                    s.sbank(2)[:, sub * 16:(sub + 1) * 16],
                    s.Rsb[0:16, sub * 128:(sub + 1) * 128],
                    s.identf[0:16, 0:16], start=(sub == 0), stop=(sub == 3),
                    is_transpose=True),
                    waits=w, inc="t" if sub == 3 else None)
            s.bank_w["s2"] = ttk
            s.tv(lambda e: e.tensor_copy(s.RTsb[:, :], s.sbank(2)[:, 0:64]),
                 waits=[ttk], inc=None, reads=(), writes=("RT",))
            trt = s.tv(lambda e: e.memset(s.st4[:, 22:23], 0.0),
                       reads=("RT",), writes=("st:22",))
            s.bank_r["s2"] = trt
            addtok = None
            for h in range(H):
                bn = "s0" if h % 2 == 0 else "s1"
                bap = s.sbank(0) if h % 2 == 0 else s.sbank(1)
                ttk = None
                for sub in range(4):
                    w = [out_evacs[-1], s.bank_r.get(bn),
                         exp_done.get(0), exp_done.get(1)] if sub == 0 else []
                    ttk = s.tt(lambda e, bap=bap, h=h, sub=sub: e.matmul(
                        bap[:, sub * 128:(sub + 1) * 128],
                        s.outT[:, h * 512 + sub * 128:h * 512 + (sub + 1) * 128],
                        s.identf[:, :], start=(sub == 0), stop=(sub == 3),
                        is_transpose=True),
                        waits=w, inc="t" if sub == 3 else None)
                s.bank_w[bn] = ttk
                for sub in range(4):
                    w = [ttk, trt, x1_ld[sub]]
                    s.tv(lambda e, bap=bap, h=h, sub=sub: e.tensor_scalar(
                        s.qstg[:, sub * 128:(sub + 1) * 128],
                        bap[:, sub * 128:(sub + 1) * 128],
                        s.RTsb[:, sub * 16 + h:sub * 16 + h + 1], None,
                        op0=ALU.mult), waits=w, inc=None,
                        reads=("RT",), writes=("qstg:%d" % sub,))
                for sub in range(4):
                    dstx = s.x1m[:, sub * E + h * 128:sub * E + (h + 1) * 128]
                    addtok = s.tv(lambda e, dstx=dstx, sub=sub: e.tensor_tensor(
                        dstx, dstx, s.qstg[:, sub * 128:(sub + 1) * 128],
                        op=ALU.add),
                        reads=("qstg:%d" % sub, "x1m:%d" % sub),
                        writes=("x1m:%d" % sub,))
                s.bank_r[bn] = addtok
            for tl_i in range(4):
                last_store = s.store(
                    s.x1_d[sq, mt * 4 + tl_i], s.x1m[:, tl_i * E:(tl_i + 1) * E],
                    "stx", waits=[addtok])
            for tl_i in range(4):
                s.x1m_tok[tl_i] = last_store
        self.arena_free = ttk
        self.kv_read = last_sums
        return last_store

    def p3(self, sq, p2_tok):
        s = self
        s.ta(lambda e: e.preload_act_table(AF.Sqrt), inc=None)
        xqT_read = None
        last_out = None
        ttk = None
        for mt in range(NMT):
            quant_last = None
            quant_tile = {}
            wf_ld = {}
            mm_tok = {}
            ev_fin = {}
            ev_last = [None]

            def wf_load(og, mt=mt, wf_ld=wf_ld):
                sl = og % 2
                wf_ld[og] = s.gload(
                    s.wfb[:, sl * 8192:(sl + 1) * 8192],
                    s.wfT_d[:, og * 8192:(og + 1) * 8192], "wfs%d" % sl,
                    waits=[s.wf_slot.get(sl), s.arena_free, p2_tok]
                          + s.setup_done_multi)

            def emit_mm(og, tl_i, wf_ld=wf_ld, mm_tok=mm_tok,
                        quant_tile=quant_tile):
                sl = og % 2
                bn = "s%d" % tl_i
                bap = s.sbank(tl_i)
                ttk = None
                for ic in range(16):
                    w = [wf_ld[og], quant_tile[tl_i], s.bank_r.get(bn)] \
                        if ic == 0 else []
                    ttk = s.tt(lambda e, bap=bap, tl_i=tl_i, ic=ic, sl=sl:
                               e.matmul(
                                   bap,
                                   s.xqTt[:, tl_i * 2048 + ic * 128:
                                          tl_i * 2048 + (ic + 1) * 128],
                                   s.wfb[:, sl * 8192 + ic * 512:
                                         sl * 8192 + (ic + 1) * 512],
                                   start=(ic == 0), stop=(ic == 15)),
                               waits=w, inc="t" if ic == 15 else None)
                s.bank_w[bn] = ttk
                s.wf_slot[sl] = ttk
                mm_tok[(og, tl_i)] = ttk

            def emit_ev(og, tl_i, mm_tok=mm_tok, ev_last=ev_last,
                        ev_tok=None):
                bn = "s%d" % tl_i
                bap = s.sbank(tl_i)
                sct = s.st4[:, 16 + tl_i:17 + tl_i]
                tvv = s.tv(lambda e, bap=bap, sct=sct: e.tensor_scalar(
                    s.vstg[:, :], bap, sct, s.wsc[:, 2:3],
                    op0=ALU.mult, op1=ALU.mult),
                    waits=[mm_tok[(og, tl_i)], getattr(s, "vstg_g", None)],
                    reads=("sct:%d" % tl_i, "wsc"), writes=("vstg",))
                s.bank_r[bn] = tvv
                dstx = s.x1m[:, tl_i * E + og * 512:tl_i * E + (og + 1) * 512]
                ev_last[0] = s.tv(lambda e, dstx=dstx: e.tensor_tensor(
                    dstx, dstx, s.vstg[:, :], op=ALU.add),
                    reads=("vstg", "x1m:%d" % tl_i),
                    writes=("x1m:%d" % tl_i,))
                ev_fin[(og, tl_i)] = ev_last[0]

            sched = {
                0: (("mm", 0, 0),),
                1: (("ev", 0, 0), ("mm", 0, 1), ("mm", 1, 0)),
                2: (("ev", 0, 1), ("ev", 1, 0), ("mm", 0, 2), ("mm", 1, 1)),
                3: (("ev", 0, 2), ("ev", 1, 1), ("mm", 0, 3), ("wf", 2, 0),
                    ("mm", 1, 2), ("ev", 0, 3), ("ev", 1, 2), ("mm", 1, 3),
                    ("wf", 3, 0), ("mm", 2, 0), ("ev", 1, 3), ("mm", 2, 1),
                    ("ev", 2, 0), ("mm", 2, 2), ("ev", 2, 1), ("mm", 2, 3),
                    ("ev", 2, 2), ("mm", 3, 0), ("ev", 2, 3), ("mm", 3, 1),
                    ("ev", 3, 0), ("mm", 3, 2), ("ev", 3, 1), ("mm", 3, 3),
                    ("ev", 3, 2), ("ev", 3, 3)),
            }
            wf_load(0)
            wf_load(1)
            for tl_i in range(4):
                t = mt * 4 + tl_i
                xap = s.x1m[:, tl_i * E:(tl_i + 1) * E]
                tld = s.load(xap, s.x1_d[sq, t], "x1m%d" % tl_i,
                             waits=[s.x1m_tok[tl_i], ("stx", s.cnt["stx"])])
                if s.debug and sq == 0 and t == 1:
                    tld = s.store(s.dbg3_d[:, :], xap, "st_dbg", waits=[tld])
                par = (t % 2) * 8
                P = "st:%d" % par
                sct = s.st4[:, 16 + tl_i:17 + tl_i]
                mean_ap, rstd_ap, tst = self.ln_stats_waitload(xap, par, tld)
                c6 = s.st4[:, par + 6:par + 7]
                c7 = s.st4[:, par + 7:par + 8]
                xk = "x1m:%d" % tl_i
                tln = s.tv(lambda e, xap=xap, m=mean_ap, r=rstd_ap:
                           e.tensor_scalar(s.scrA[:, :], xap, m, r,
                                           op0=ALU.subtract, op1=ALU.mult),
                           waits=[tst, s.scr_tok["A"]],
                           reads=(xk, P + "m", P + "v"), writes=("scrA",))
                cA = s.st4[:, par + 1:par + 2]
                cR = s.st4[:, par + 2:par + 3]
                s.tv(lambda e: e.tensor_scalar(s.scrB[:, :], s.scrA[:, :],
                                               -1.0, None, op0=ALU.mult),
                     waits=[tln, s.scr_tok["B"]], inc=None,
                     reads=("scrA",), writes=("scrB",))
                s.tv(lambda e: e.tensor_tensor(s.scrB[:, :], s.scrA[:, :],
                                               s.scrB[:, :], op=ALU.max),
                     inc=None, reads=("scrA", "scrB"), writes=("scrB",))
                for half in (1024, 512, 256, 128, 64):
                    s.tv(lambda e, half=half: e.tensor_tensor(
                        s.scrB[:, 0:half], s.scrB[:, 0:half],
                        s.scrB[:, half:2 * half], op=ALU.max), inc=None,
                        reads=("scrB",), writes=("scrB",))
                tp = s.tv(lambda e, c6=c6: e.pool_max(c6, s.scrB[:, 0:64]),
                          reads=("scrB",), writes=(P + "p",))
                ta = s.ta(lambda e, c6=c6, cA=cA: e.activation(
                    cA, c6, AF.Identity), waits=[tp],
                    reads=(P + "p",), writes=(P + "A",))
                cC = s.st4[:, par + 0:par + 1]
                s.tv(lambda e, cA=cA, cC=cC: e.tensor_scalar(
                    cC, cA, 1e-8, None, op0=ALU.max), waits=[ta], inc=None,
                    reads=(P + "A",), writes=(P + "C",))
                s.tv(lambda e, cC=cC, sct=sct: e.tensor_copy(sct, cC),
                     waits=[getattr(s, "dbg_tok", None)], inc=None,
                     reads=(P + "C",), writes=("sct:%d" % tl_i,))
                s.tv(lambda e, cC=cC, cR=cR: e.reciprocal(cR, cC), inc=None,
                     reads=(P + "C",), writes=(P + "R",))
                tvv = s.tv(lambda e, cR=cR: e.tensor_scalar(
                    s.scrB[:, :], s.scrA[:, :], cR, 127.0,
                    op0=ALU.mult, op1=ALU.mult),
                    reads=("scrA", P + "R"), writes=("scrB",))
                s.scr_tok["B"] = tvv
                ta = s.ta(lambda e: e.activation(s.i32t[:, :], s.scrB[:, :],
                                                 AF.Identity),
                          waits=[tvv, s.i32_tok], reads=("scrB",),
                          writes=("i32",))
                s.i32_tok = s.tv(lambda e: e.tensor_copy(s.bstage[:, :],
                                                         s.i32t[:, :]),
                                 waits=[ta, s.bstage_tok],
                                 reads=("i32",), writes=("bst",))
                tr = s.tv(lambda e: e.tensor_tensor(s.scrA[:, :], s.scrB[:, :],
                                                    s.bstage[:, :],
                                                    op=ALU.subtract),
                          reads=("scrB", "bst"), writes=("scrA",))
                s.scr_tok["A"] = tr
                s.ta(lambda e: e.activation(s.sphi[:, :], s.scrA[:, :], AF.Sign,
                                            bias=s.cst[:, 0:1]),
                     waits=[tr, s.sphi_tok], inc=None,
                     reads=("scrA", "cst"), writes=("sph",))
                ta = s.ta(lambda e: e.activation(s.splo[:, :], s.scrA[:, :],
                                                 AF.Sign, bias=s.cst[:, 1:2]),
                          waits=[s.splo_tok],
                          reads=("scrA", "cst"), writes=("spl",))
                tvv = s.tv(lambda e: e.tensor_tensor(s.scrB[:, :], s.sphi[:, :],
                                                     s.splo[:, :], op=ALU.add),
                           waits=[ta], reads=("sph", "spl"), writes=("scrB",))
                s.scr_tok["B"] = tvv
                s.splo_tok = tvv
                tvv = s.tv(lambda e: e.tensor_scalar(s.scrA[:, :], s.scrB[:, :],
                                                     0.5, None, op0=ALU.mult),
                           reads=("scrB",), writes=("scrA",))
                s.scr_tok["A"] = tvv
                txq = s.tv(lambda e: e.tensor_tensor(s.bstage[:, :],
                                                     s.bstage[:, :], s.scrA[:, :],
                                                     op=ALU.add),
                           reads=("bst", "scrA"), writes=("bst",))
                items = [s.bstage[:, ic * 128:(ic + 1) * 128]
                         for ic in range(16)]

                def dst_fn(bi, pbap, ttok, tl_i=tl_i):
                    return s.tv(lambda e, pbap=pbap, bi=bi, tl_i=tl_i:
                                e.tensor_copy(
                                    s.xqTt[:, tl_i * 2048 + bi * 1024:
                                           tl_i * 2048 + (bi + 1) * 1024],
                                    pbap[:, 0:1024]),
                                waits=[ttok, xqT_read, s.arena_free],
                                reads=(), writes=("xqT:%d:%d" % (tl_i, bi),))

                dbg_st = None
                if s.debug and sq == 0:
                    dbg_st = s.store(s.dbg2_d[t], s.bstage[:, :], "st_dbg", waits=[txq])
                toks = s.bf16_transpose_batch(items, dst_fn,
                                              extra_waits=[txq, s.bstage_tok])
                s.bstage_tok = dbg_st if dbg_st is not None else toks[-1]
                s.sphi_tok = tvv
                s.splo_tok = tvv
                s.scr_tok["A"] = txq
                s.scr_tok["B"] = txq
                quant_tile[tl_i] = toks[-1]
                quant_last = toks[-1]
                for item in sched[tl_i]:
                    if item[0] == "mm":
                        emit_mm(item[1], item[2])
                    elif item[0] == "ev":
                        emit_ev(item[1], item[2])
                    else:
                        wf_load(item[1])
            if s.debug and sq == 0 and mt == 0:
                s.dbg_tok = s.store(s.dbg_d[:, 68:72], s.st4[:, 16:20],
                                    "st_dbg", waits=[quant_last])
            ttk = mm_tok[(3, 3)]
            xqT_read = ttk
            for tl_i in range(4):
                t = mt * 4 + tl_i
                last_out = s.store(
                    s.y_h[sq, t * 128:(t + 1) * 128, :],
                    s.x1m[:, tl_i * E:(tl_i + 1) * E],
                    "sty" if tl_i == 0 else "sty%d" % tl_i,
                    waits=[ev_fin[(3, tl_i)]])
                s.x1m_tok[tl_i] = last_out
        self.arena_free = ttk
        return last_out

    def build(self, phases="all"):
        self.setup()
        if phases == "all":
            for sq in range(NSEQ):
                p1_tok = self.p1(sq)
                p2_tok = self.p2(sq, p1_tok)
                self.p3(sq, p2_tok)
        for sn in ("st_wf", "st_jh", "st_jl", "stq_h", "stq_l", "stx",
                   "sty", "sty1", "sty2", "sty3", "st_dbg"):
            total = self.cnt[sn]
            if total:
                self.emit("a", lambda e, sn=sn, v=total: e.wait_ge(
                    self.sems[sn], v), inc=None, drain=False)
        with self.nc.Block() as block:
            for ename, attr in (("t", "tensor"), ("v", "vector"),
                                ("a", "scalar"), ("g", "gpsimd"), ("y", "sync")):
                ops = self.ops[ename]

                def body(eng, ops=ops):
                    for op in ops:
                        op(eng)

                getattr(block, attr)(body)
        return self.nc


_CACHE = {}


def _get_nc(debug=False, phases="all"):
    key = (debug, phases)
    if key not in _CACHE:
        _CACHE[key] = B(debug=debug).build(phases=phases)
    return _CACHE[key]


def kernel(L0_attn_x, L0_attn_Wq, L0_attn_bq, L0_attn_Wk, L0_attn_bk,
           L0_attn_Wv, L0_attn_bv, L0_ffn_weight):
    x = np.ascontiguousarray(np.asarray(L0_attn_x, dtype=np.float32))
    wq = np.ascontiguousarray(np.asarray(L0_attn_Wq, dtype=np.float32))
    bq = np.ascontiguousarray(np.asarray(L0_attn_bq, dtype=np.float32))
    wk = np.ascontiguousarray(np.asarray(L0_attn_Wk, dtype=np.float32))
    bk = np.ascontiguousarray(np.asarray(L0_attn_bk, dtype=np.float32))
    wv = np.ascontiguousarray(np.asarray(L0_attn_Wv, dtype=np.float32))
    bv = np.ascontiguousarray(np.asarray(L0_attn_bv, dtype=np.float32))
    wf = np.ascontiguousarray(np.asarray(L0_ffn_weight, dtype=np.float32))
    nc = _get_nc()
    in_maps = []
    for c in range(8):
        in_maps.append({"x": x[2 * c:2 * c + 2], "wq": wq, "bq": bq, "wk": wk,
                        "bk": bk, "wv": wv, "bv": bv, "wf": wf})
    res = run_bass_kernel_spmd(nc, in_maps, core_ids=list(range(8)))
    out = np.concatenate([r["y"] for r in res.results], axis=0)
    return out.reshape(16, S, E)
